# version 50
# baseline (speedup 1.0000x reference)
import contextlib
import math
import numpy as np
import ml_dtypes
import concourse.bass as bass
import concourse.mybir as mybir
from concourse.bass_utils import run_bass_kernel_spmd

F32 = mybir.dt.float32
BF16 = mybir.dt.bfloat16
AF = mybir.ActivationFunctionType
ALU = mybir.AluOpType
AX = mybir.AxisListType

B = 2
SEQ = 4096
D = 2048
DH = 128
HPC = 2
INTER = 5632
NIC = INTER // 128
NDC = D // 128
TOWN = 1024
TALL = 8192
NCORES = 8
SCALE = 1.0 / math.sqrt(DH)
EPS = 1e-5

ENGS = ("sync", "scalar", "vector", "tensor", "gpsimd")


class Bld:

    def __init__(self, nc):
        self.nc = nc
        self.streams = {e: [] for e in ENGS}
        self.C = {}
        self.wmax = {e: {} for e in ENGS}
        self.semnames = []
        self.rel = {}

    def sem(self, name):
        self.C[name] = 0
        self.semnames.append(name)

    def wt(self, eng, sname, cnt):
        if cnt is None or cnt <= 0:
            return
        hw = self.wmax[eng].get(sname, -1)
        if cnt <= hw:
            return
        self.wmax[eng][sname] = cnt

        def fn(e, sems, s=sname, c=cnt):
            e.wait_ge(sems[s], c)
        self.streams[eng].append(fn)

    def we(self, eng, ev):
        if ev is not None:
            self.wt(eng, ev[0], ev[1])

    def use(self, eng, res):
        self.we(eng, self.rel.get(res))

    def release(self, res, sname):
        self.rel[res] = (sname, self.C[sname])

    def ins(self, eng, fn, inc=None, inc_by=1):
        out = None
        if inc is not None:
            self.C[inc] += inc_by
            out = self.C[inc]

            def wrapped(e, sems, fn=fn, s=inc, n=inc_by):
                fn(e, sems).then_inc(sems[s], n)
            self.streams[eng].append(wrapped)
        else:
            def wrapped2(e, sems, fn=fn):
                fn(e, sems)
            self.streams[eng].append(wrapped2)
        return out


def build_nc(debug=False):
    nc = bass.Bass("TRN2", target_bir_lowering=False)
    bld = Bld(nc)
    ins, wt, use, release = bld.ins, bld.wt, bld.use, bld.release
    C = bld.C

    def din(name, shape, dtype):
        return nc.dram_tensor(name, shape, dtype, kind="ExternalInput").ap()

    x_own = din("x_own", [TOWN, D], F32)
    wqT = din("wqT", [128, NDC * 256], F32)
    wkT = din("wkT", [128, NDC * 256], F32)
    wvT = din("wvT", [128, NDC * 256], F32)
    woT_in = din("woT", [HPC * DH, D], F32)
    gateT = din("gateT", [NIC * 128, D], F32)
    upT = din("upT", [NIC * 128, D], F32)
    downT = din("downT", [NDC * 128, INTER], F32)
    gs_in = din("gs", [128, NIC], F32)
    us_in = din("us", [128, NIC], F32)
    ds_in = din("ds", [128, NDC], F32)
    masks_in = din("masks", [128, 4 * 512], BF16)
    ident_in = din("ident", [128, 128], BF16)
    onesc_in = din("ones_col", [128, 1], BF16)
    onesr_in = din("ones_row", [1, 128], F32)
    onesrb_in = din("ones_row_bf", [1, 128], BF16)
    sel32_in = din("sel32", [128, 128], BF16)

    out_parts = nc.dram_tensor("out_parts", [8, 128, D], F32, kind="ExternalOutput").ap()

    xn_src = nc.dram_tensor("xn_src", [D, TOWN], BF16, kind="Internal").ap()
    xn_all = nc.dram_tensor("xn_all", [NCORES * D, TOWN], BF16, kind="Internal",
                            addr_space="Shared").ap()
    o_part = [nc.dram_tensor(f"o_part{c}", [1024, D], BF16, kind="Internal").ap()
              for c in range(8)]
    rs_dst = [nc.dram_tensor(f"rs_dst{c}", [128, D], BF16, kind="Internal").ap()
              for c in range(8)]
    bar_src = nc.dram_tensor("bar_src", [1, 64], F32, kind="Internal").ap()
    bar_dst = [nc.dram_tensor(f"bar_dst{i}", [1, 64], F32, kind="Internal").ap()
               for i in range(6)]

    SB_BASE = 16512

    def sb(name, shape, dtype, off):
        return nc.alloc_sbuf_tensor_at(name, shape, dtype, offset=SB_BASE + off)

    masks = sb("masks", [128, 4 * 512], BF16, 0)
    ident = sb("ident", [128, 128], BF16, 4096)
    ones_col = sb("ones_col", [128, 1], BF16, 4352)
    ones_row = sb("ones_row", [1, 128], F32, 4384)
    gs = sb("gs", [128, NIC], F32, 4896)
    us = sb("us", [128, NIC], F32, 5088)
    ds = sb("ds", [128, NDC], F32, 5280)
    bias_nc = sb("bias_nc", [128, 4], F32, 5344)
    stats6 = sb("stats6", [128, 24], F32, 5408)
    mv2 = sb("mv2", [128, 16], F32, 5504)
    meanc = sb("meanc", [128, 8], F32, 5568)
    varc = sb("varc", [128, 8], F32, 5600)
    stdc = sb("stdc", [128, 8], F32, 5632)
    rstdc = sb("rstdc", [128, 8], F32, 5664)
    nmrc = sb("nmrc", [128, 8], F32, 5696)
    eps_col = sb("eps_col", [128, 1], F32, 6112)
    nmax = sb("nmax", [1, 64], F32, 5728)
    nmm = sb("nmm", [1, 8], F32, 5984)
    c2s = sb("c2s", [1, 4], F32, 6016)
    cs_ = sb("cs_", [1, 4], F32, 6048)
    negc = sb("negc", [1, 4], F32, 6080)
    ffn_tok = sb("ffn_tok", [128, 8 * D], BF16, 8000)
    woT_sb = sb("woT_sb", [128, HPC * D], BF16, 40768)
    sel32 = sb("sel32", [128, 128], BF16, 6144)
    ones_row_bf = sb("ones_row_bf", [1, 128], BF16, 6400)
    recip_bf = sb("recip_bf", [1, 512], BF16, 6656)
    pt = sb("pt", [128, 6 * 512], BF16, 48960)
    attnT = sb("attnT", [128, 4 * 512], BF16, 55104)
    recip = sb("recip", [1, 512], F32, 59200)
    rb_sb = sb("rb_sb", [128, 512], F32, 61248)
    o_stage = sb("o_stage", [128, 2 * D], BF16, 63296)
    dtmp = sb("dtmp", [128, 2 * 512], BF16, 48960)
    sgt = sb("sgt", [128, 2 * 512], BF16, 51008)
    vtmp = sb("vtmp", [128, 2 * 512], BF16, 48960)
    dstrip2 = sb("dstrip2", [128, NIC * 128], BF16, 53056)

    xt = [sb(f"xt{i}", [128, D], F32, 72512 + i * 8192) for i in range(2)]
    xn_tok = [sb(f"xn_tok{i}", [128, D], BF16, 88896 + i * 4096) for i in range(2)]
    xnT = sb("xnT", [128, NDC * TOWN], BF16, 180064)
    h_all = sb("h_all", [128, NIC * 512], BF16, 72512)
    gstrip = [sb(f"gstrip{i}", [128, NDC * 128], BF16, 117568 + i * 4096) for i in range(4)]
    ustrip = [sb(f"ustrip{i}", [128, NDC * 128], BF16, 133952 + i * 4096) for i in range(4)]
    dstrip = [sb(f"dstrip{i}", [128, NIC * 128], BF16, 150336 + i * 11264) for i in range(2)]
    dstrip.append(dstrip2)
    dstrip.append(sb("dstrip3", [128, NIC * 128], BF16, 117568))
    dstrip.append(sb("dstrip4", [128, NIC * 128], BF16, 133952))
    xn_in = [sb(f"xn_in{i}", [128, 8 * 512], BF16, 72512 + i * 8192) for i in range(3)]
    qt = sb("qt", [128, HPC * TALL], BF16, 97088)
    kt_ = sb("kt", [128, HPC * TALL], BF16, 129856)
    vtok = sb("vtok", [128, 64 * 256], BF16, 162624)
    wq_sb = sb("wqsb", [128, NDC * 256], BF16, 180064)
    wk_sb = sb("wksb", [128, NDC * 256], BF16, 188256)
    wv_sb = sb("wvsb", [128, NDC * 256], BF16, 196448)
    sq_sb = sb("sq_sb", [128, 4096], BF16, 51008)
    rs_in = sb("rs_in", [128, 2 * D], BF16, 72512)
    x_ep = [sb(f"x_ep{i}", [128, D], F32, 195392 + i * 8192) for i in range(2)]
    oout = [sb(f"oout{i}", [128, D], F32, 80704 + i * 8192) for i in range(2)]

    stack = contextlib.ExitStack()
    p = [stack.enter_context(nc.psum_tensor(f"p{i}", [128, 512], F32)) for i in range(6)]
    cm6 = nc.psum_tensor("p6", [128, 1024], BF16)
    cm7 = nc.psum_tensor("p7", [128, 1024], BF16)
    p6 = cm6.__enter__()
    p7 = cm7.__enter__()

    for s in ("ldq", "gpq", "stq", "ostq", "te", "se", "ve", "gp",
              "cc0", "cc1", "cc2", "cc3", "cc4", "cc5", "cc6", "cc7", "cc8"):
        bld.sem(s)

    def gdma(dst_ap, src_ap):
        return ins("gpsimd", lambda e, sems, d=dst_ap, s=src_ap: e.dma_start(out=d, in_=s),
                   inc="gpq", inc_by=16)

    nbar = [0]

    def barrier():
        i = nbar[0]
        nbar[0] += 1
        ins("gpsimd", lambda e, sems, ii=i: e.collective_compute(
            "AllReduce", ALU.add, [list(range(NCORES))],
            ins=[bar_src], outs=[bar_dst[ii]]), inc="gp")
        wt("gpsimd", "gp", C["gp"])

    gdma(bar_src, onesr_in[0:1, 0:64])
    gdma(sel32[:, :], sel32_in)
    gdma(ones_row_bf[:, :], onesrb_in)
    gdma(masks[:, :], masks_in)
    gdma(ident[:, :], ident_in)
    gdma(ones_col[:, :], onesc_in)
    gdma(ones_row[:, :], onesr_in)
    gdma(gs[:, :], gs_in)
    gdma(us[:, :], us_in)
    gdma(ds[:, :], ds_in)
    c_consts = C["gpq"]
    wt("gpsimd", "gpq", c_consts)
    barrier()

    v1b = ins("vector", lambda e, sems: e.memset(eps_col[:, :], EPS), inc="ve")
    ev = None
    for tc in range(8):
        use("sync", f"xt{tc % 2}")
        ld = ins("sync", lambda e, sems, t=tc: e.dma_start(
            out=xt[t % 2][:, :], in_=x_own[t * 128:(t + 1) * 128, :]),
            inc="ldq", inc_by=16)
        wt("vector", "ldq", ld)
        last = None
        for c6 in range(4):
            last = ins("vector", lambda e, sems, t=tc, c=c6: e.bn_stats(
                stats6[:, c * 6:(c + 1) * 6], xt[t % 2][:, c * 512:(c + 1) * 512]),
                inc="ve" if c6 == 3 else None)
        wt("vector", "ve", last)
        va = ins("vector", lambda e, sems, t=tc: e.bn_aggr(
            mv2[:, t * 2:(t + 1) * 2], stats6[:, :]), inc="ve")
        wt("scalar", "ve", va)
        s1 = ins("scalar", lambda e, sems, t=tc: e.activation(
            stdc[:, t:t + 1], mv2[:, 2 * t + 1:2 * t + 2], AF.Sqrt,
            bias=eps_col[:, 0:1]), inc="se")
        wt("vector", "se", s1)
        v2 = ins("vector", lambda e, sems, t=tc: e.reciprocal(
            rstdc[:, t:t + 1], stdc[:, t:t + 1]), inc="ve")
        wt("vector", "ve", v2)
        v3 = ins("vector", lambda e, sems, t=tc: e.scalar_tensor_tensor(
            nmrc[:, t:t + 1], mv2[:, 2 * t:2 * t + 1], -1.0, rstdc[:, t:t + 1],
            ALU.mult, ALU.mult), inc="ve")
        wt("vector", "ve", v3)
        use("vector", f"xn_tok{tc % 2}")
        se_ = ins("vector", lambda e, sems, t=tc: e.tensor_scalar(
            xn_tok[t % 2][:, :], xt[t % 2][:, :], rstdc[:, t:t + 1],
            nmrc[:, t:t + 1], ALU.mult, ALU.add), inc="ve")
        release(f"xt{tc % 2}", "ve")
        for half, bank, bname in ((0, p6, "p6"), (1, p7, "p7")):
            wt("tensor", "gpq", c_consts)
            wt("tensor", "ve", se_)
            use("tensor", bname)
            tl = None
            for k in range(8):
                dc = half * 8 + k
                tl = ins("tensor", lambda e, sems, t=tc, d=dc, b=bank, kk=k: e.transpose(
                    b[:, kk * 128:(kk + 1) * 128], xn_tok[t % 2][:, d * 128:(d + 1) * 128],
                    ident[:, :]), inc="te" if k == 7 else None)
            wt("vector", "te", tl)
            dst = xnT[:].rearrange("q (dc t) -> q dc t", dc=NDC)[
                :, half * 8:(half + 1) * 8, tc * 128:(tc + 1) * 128]
            src = bank[:].rearrange("q (k c) -> q k c", k=8)
            ev = ins("vector", lambda e, sems, d=dst, s=src: e.tensor_copy(d, s), inc="ve")
            release(bname, "ve")
        release(f"xn_tok{tc % 2}", "ve")
    xnT_done = ("ve", ev)
    bld.we("scalar", xnT_done)
    xnsrc_ap = xn_src.rearrange("(dc q) t -> q dc t", q=128)
    st1 = ins("scalar", lambda e, sems: e.dma_start(
        out=xnsrc_ap, in_=xnT[:].rearrange("q (dc t) -> q dc t", dc=NDC)),
        inc="stq", inc_by=16)

    def strip_dma(kind, ic, slot):
        if kind == "g":
            src = gateT[ic * 128:(ic + 1) * 128, :]
            dst = gstrip[slot][:, :]
        elif kind == "u":
            src = upT[ic * 128:(ic + 1) * 128, :]
            dst = ustrip[slot][:, :]
        else:
            src = downT[ic * 128:(ic + 1) * 128, :]
            dst = dstrip[slot][:, :]
        use("gpsimd", f"{kind}slot{slot}")
        return ins("gpsimd", lambda e, sems, s=src, d=dst: e.dma_start(out=d, in_=s),
                   inc="gpq", inc_by=16)

    gu_lds = {}
    for ic in range(4):
        gu_lds[("g", ic)] = strip_dma("g", ic, ic % 4)
        gu_lds[("u", ic)] = strip_dma("u", ic, ic % 4)

    te_down_last = 0
    te_after_gu = 0
    for th in range(2):
        for ic in range(NIC):
            kg, ku = ("g", ic if th == 0 else 100 + ic), ("u", ic if th == 0 else 100 + ic)
            if kg not in gu_lds:
                gu_lds[kg] = strip_dma("g", ic, ic % 4)
                gu_lds[ku] = strip_dma("u", ic, ic % 4)
            gb, ub = (p[0], p[1]) if ic % 2 == 0 else (p[2], p[3])
            gbn, ubn = ("p0", "p1") if ic % 2 == 0 else ("p2", "p3")
            wt("tensor", "gpq", gu_lds[kg])
            wt("tensor", "gpq", gu_lds[ku])
            if th == 0 and ic == 0:
                bld.we("tensor", xnT_done)
            use("tensor", gbn)
            use("tensor", ubn)
            tl = None
            for dc in range(NDC):
                rhs = xnT[:, dc * TOWN + th * 512: dc * TOWN + th * 512 + 512]
                ins("tensor", lambda e, sems, d=dc, g=gb, i=ic, r=rhs: e.matmul(
                    g[:, :], gstrip[i % 4][:, d * 128:(d + 1) * 128], r,
                    start=(d == 0), stop=(d == NDC - 1)))
                tl = ins("tensor", lambda e, sems, d=dc, u=ub, i=ic, r=rhs: e.matmul(
                    u[:, :], ustrip[i % 4][:, d * 128:(d + 1) * 128], r,
                    start=(d == 0), stop=(d == NDC - 1)),
                    inc="te" if dc == NDC - 1 else None)
            release(f"gslot{ic % 4}", "te")
            release(f"uslot{ic % 4}", "te")
            wt("scalar", "gpq", c_consts)
            wt("scalar", "te", tl)
            use("scalar", f"sgt{ic % 2}")
            sl = ins("scalar", lambda e, sems, g=gb, i=ic: e.activation(
                sgt[:, (i % 2) * 512:(i % 2) * 512 + 512], g[:, :], AF.Silu,
                scale=gs[:, i:i + 1]), inc="se")
            release(gbn, "se")
            wt("vector", "se", sl)
            wt("vector", "te", tl)
            if th == 1 and ic == 0:
                wt("vector", "te", te_down_last)
            vl = ins("vector", lambda e, sems, u=ub, i=ic: e.scalar_tensor_tensor(
                h_all[:, i * 512:(i + 1) * 512], u[:, :], us[:, i:i + 1],
                sgt[:, (i % 2) * 512:(i % 2) * 512 + 512], ALU.mult, ALU.mult), inc="ve")
            release(ubn, "ve")
            release(f"sgt{ic % 2}", "ve")
        h_done = C["ve"]
        te_after_gu = C["te"]
        if th == 0:
            d_pre = {}
            bld.rel["dslot3"] = ("te", C["te"])
            bld.rel["dslot4"] = ("te", C["te"])
            for do in range(5):
                d_pre[do] = strip_dma("d", do, do)
            wt("gpsimd", "gpq", C["gpq"])
            wt("gpsimd", "stq", st1)
            barrier()
            ins("gpsimd", lambda e, sems: e.collective_compute(
                "AllGather", ALU.bypass, [list(range(NCORES))],
                ins=[xn_src], outs=[xn_all]), inc="cc0")
            wt("gpsimd", "cc0", 1)
        if debug:
            dbgh = nc.dram_tensor(f"dbg_h{th}", [128, NIC * 512], BF16,
                                  kind="ExternalOutput").ap()
            wt("scalar", "ve", h_done)
            hd = ins("scalar", lambda e, sems, d=dbgh: e.dma_start(out=d, in_=h_all[:, :]),
                     inc="stq", inc_by=16)
            wt("vector", "stq", hd)
        d_lds = dict(d_pre) if th == 0 else {}

        def dslot_of(i, th=th):
            if th == 0:
                return i if i < 5 else (i - 5) % 3
            return i % 3
        for do in range(NDC):
            while len(d_lds) < min(do + 3, NDC):
                nxt = len(d_lds)
                d_lds[nxt] = strip_dma("d", nxt, dslot_of(nxt))
            bank, bn_ = (p[0], "p0") if do % 2 == 0 else (p[1], "p1")
            wt("tensor", "gpq", d_lds[do])
            wt("tensor", "ve", h_done)
            use("tensor", bn_)
            tl = None
            for kc in range(NIC):
                tl = ins("tensor", lambda e, sems, k=kc, b=bank, d=dslot_of(do): e.matmul(
                    b[:, :], dstrip[d][:, k * 128:(k + 1) * 128],
                    h_all[:, k * 512:(k + 1) * 512],
                    start=(k == 0), stop=(k == NIC - 1)),
                    inc="te" if kc == NIC - 1 else None)
            release(f"dslot{dslot_of(do)}", "te")
            if th == 0 and do == 3:
                for s_ in ("gslot0", "gslot1", "gslot2"):
                    release(s_, "te")
            if th == 0 and do == 4:
                for s_ in ("uslot0", "uslot1", "uslot2"):
                    release(s_, "te")
            te_down_last = tl
            wt("scalar", "te", tl)
            use("scalar", f"dtmp{do % 2}")
            sl = ins("scalar", lambda e, sems, b=bank, d=do: e.activation(
                dtmp[:, (d % 2) * 512:(d % 2) * 512 + 512], b[:, :], AF.Copy,
                scale=ds[:, d:d + 1]), inc="se")
            release(bn_, "se")
            tb, tbn = (p6, "p6") if do % 2 == 0 else (p7, "p7")
            wt("tensor", "se", sl)
            use("tensor", tbn)
            tl2 = None
            for j in range(4):
                tl2 = ins("tensor", lambda e, sems, d=do, jj=j, b=tb: e.transpose(
                    b[:, jj * 128:(jj + 1) * 128],
                    dtmp[:, (d % 2) * 512 + jj * 128:(d % 2) * 512 + (jj + 1) * 128],
                    ident[:, :]), inc="te" if j == 3 else None)
            wt("vector", "te", tl2)
            dstv = ffn_tok[:].rearrange("q (t c) -> q t c", t=8)[
                :, th * 4:(th + 1) * 4, do * 128:(do + 1) * 128]
            srcv = tb[:].rearrange("q (j c) -> q j c", j=8)[:, 0:4, :]
            ins("vector", lambda e, sems, d=dstv, s=srcv: e.tensor_copy(d, s), inc="ve")
            release(tbn, "ve")
            release(f"dtmp{do % 2}", "ve")
    te_after_ffn = C["te"]

    wt("gpsimd", "cc0", 1)
    wt("gpsimd", "te", te_after_gu)
    wq_ld = gdma(wq_sb[:, :], wqT)
    wk_ld = gdma(wk_sb[:, :], wkT)
    wv_ld = gdma(wv_sb[:, :], wvT)
    wo_ld = gdma(woT_sb[:].rearrange("q (hc c) -> q hc c", hc=HPC),
                 woT_in.rearrange("(hc q) c -> q hc c", q=128))

    cm7.__exit__(None, None, None)
    cm6.__exit__(None, None, None)
    cm6f = nc.psum_tensor("p6fa", [128, 512], F32)
    cm7f = nc.psum_tensor("p7fa", [128, 512], F32)
    p6f = cm6f.__enter__()
    p7f = cm7f.__enter__()
    bld.rel["p6f"] = bld.rel.get("p6")
    bld.rel["p7f"] = bld.rel.get("p7")

    xn_all_v = xn_all.rearrange("(r dc q) t -> r q dc t", r=NCORES, q=128)
    it = 0
    hbuf = 0
    nq = []
    nstate = {"te_prev": 0, "sqd": {}, "vl": None}

    def drain_norm(k):
        for _ in range(k):
            if not nq:
                return
            kind, s8, arg = nq.pop(0)
            qk, nbb, nhc = s8 // 4, (s8 % 4) // 2, s8 % 2
            if kind == "sq":
                srcn = (qt if qk == 0 else kt_)[:, nhc * TALL + nbb * SEQ:
                                                nhc * TALL + (nbb + 1) * SEQ]
                wt("scalar", "ve", arg)
                wt("scalar", "te", nstate["te_prev"])
                nstate["sqd"][s8] = ins("scalar", lambda e, sems, sr=srcn: e.activation(
                    sq_sb[:, :], sr, AF.Square), inc="se")
            else:
                tt = arg
                bank, bn2 = (p[4], "p4") if tt % 2 == 0 else (p[5], "p5")
                wt("tensor", "se", nstate["sqd"][s8])
                wt("tensor", "gpq", c_consts)
                use("tensor", bn2)
                tl = ins("tensor", lambda e, sems, t=tt, b=bank: e.matmul(
                    b[0:1, :], ones_col[:, :], sq_sb[:, t * 512:(t + 1) * 512],
                    start=True, stop=True), inc="te")
                if tt == 7:
                    nstate["te_prev"] = tl
                wt("vector", "te", tl)
                nstate["vl"] = ins("vector", lambda e, sems, s=s8, t=tt, b=bank: e.reduce_max(
                    nmax[0:1, s * 8 + t: s * 8 + t + 1], b[0:1, :], axis=AX.X), inc="ve")
                release(bn2, "ve")

    for r8 in range(NCORES):
        for c2 in range(2):
            lds = []
            for half in range(2):
                buf = hbuf % 3
                hbuf += 1
                if it == 0 and half == 0:
                    wt("sync", "cc0", 1)
                    wt("sync", "te", te_after_ffn)
                use("sync", f"xn_in{buf}")
                lds.append((buf, ins("sync", lambda e, sems, r=r8, c=c2, hh=half,
                                     b=buf: e.dma_start(
                    out=xn_in[b][:].rearrange("q (dc t) -> q dc t", dc=8),
                    in_=xn_all_v[r, :, hh * 8:(hh + 1) * 8, c * 512:(c + 1) * 512]),
                    inc="ldq", inc_by=16)))
            par = it % 2
            qb = ((p[0], "p0"), (p[1], "p1")) if par == 0 else ((p[4], "p4"), (p[5], "p5"))
            kb = ((p[2], "p2"), (p[3], "p3")) if par == 0 else ((p6f, "p6f"), (p7f, "p7f"))
            wt("tensor", "gpq", wk_ld)
            for _, n_ in qb + kb:
                use("tensor", n_)
            tl = None
            for half in range(2):
                buf, ld = lds[half]
                wt("tensor", "ldq", ld)
                for d8 in range(8):
                    dc = half * 8 + d8
                    for wi, (w_sb, bks) in enumerate(((wq_sb, qb), (wk_sb, kb))):
                        for j in range(2):
                            bk = bks[j][0]
                            tl = ins("tensor", lambda e, sems, d=dc, d8_=d8, jj=j, b=bk,
                                     bf=buf, w=w_sb: e.matmul(
                                b[:, :],
                                w[:, d * 256 + jj * 128:d * 256 + (jj + 1) * 128],
                                xn_in[bf][:, d8_ * 512:(d8_ + 1) * 512],
                                start=(d == 0), stop=(d == NDC - 1)),
                                inc="te" if (d8 == 7 and wi == 1 and j == 1) else None)
                release(f"xn_in{buf}", "te")
            wt("vector", "te", tl)
            for qk, (dst_sb, bks) in enumerate(((qt, qb), (kt_, kb))):
                for j in range(2):
                    bk, bn2 = bks[j]
                    dvv = dst_sb[:].rearrange("q (a c) -> q a c", c=128)
                    a0 = j * 64 + c2 * 32 + r8
                    dv = dvv[:, a0:a0 + 25:8, :]
                    vl = ins("vector", lambda e, sems, d=dv, b=bk: e.tensor_copy(
                        d, b[:].rearrange("q (b c) -> q b c", c=128)), inc="ve")
                    release(bn2, "ve")
            it += 1
    for s8 in range(8):
        nq.append(("sq", s8, C["ve"]))
        for tt in range(8):
            nq.append(("mm", s8, tt))

    cm7f.__exit__(None, None, None)
    cm6f.__exit__(None, None, None)
    cm6b = nc.psum_tensor("p6b", [128, 1024], BF16)
    cm7b = nc.psum_tensor("p7b", [128, 1024], BF16)
    p6b = cm6b.__enter__()
    p7b = cm7b.__enter__()
    bld.rel["p6b"] = bld.rel.get("p6f")
    bld.rel["p7b"] = bld.rel.get("p7f")

    vtr = 0
    for r8 in range(NCORES):
        for c2 in range(2):
            lds = []
            for half in range(2):
                buf = hbuf % 3
                hbuf += 1
                use("sync", f"xn_in{buf}")
                lds.append((buf, ins("sync", lambda e, sems, r=r8, c=c2, hh=half,
                                     b=buf: e.dma_start(
                    out=xn_in[b][:].rearrange("q (dc t) -> q dc t", dc=8),
                    in_=xn_all_v[r, :, hh * 8:(hh + 1) * 8, c * 512:(c + 1) * 512]),
                    inc="ldq", inc_by=16)))
            b0, b1 = (p[0], p[1]) if it % 2 == 0 else (p[2], p[3])
            n0, n1 = ("p0", "p1") if it % 2 == 0 else ("p2", "p3")
            wt("tensor", "gpq", wv_ld)
            use("tensor", n0)
            use("tensor", n1)
            tl = None
            for half in range(2):
                buf, ld = lds[half]
                wt("tensor", "ldq", ld)
                for d8 in range(8):
                    dc = half * 8 + d8
                    for j, bk in ((0, b0), (1, b1)):
                        tl = ins("tensor", lambda e, sems, d=dc, d8_=d8, jj=j, b=bk,
                                 bf=buf: e.matmul(
                            b[:, :],
                            wv_sb[:, d * 256 + jj * 128:d * 256 + (jj + 1) * 128],
                            xn_in[bf][:, d8_ * 512:(d8_ + 1) * 512],
                            start=(d == 0), stop=(d == NDC - 1)),
                            inc="te" if (d8 == 7 and j == 1) else None)
                release(f"xn_in{buf}", "te")
            wt("vector", "te", tl)
            for j, (bk, bn2) in enumerate(((b0, n0), (b1, n1))):
                use("vector", f"vtmp{j}")
                vl = ins("vector", lambda e, sems, b=bk, jj=j: e.tensor_copy(
                    vtmp[:, jj * 512:(jj + 1) * 512], b[:, :]), inc="ve")
                release(bn2, "ve")
                tb, tbn = (p6b, "p6b") if vtr % 2 == 0 else (p7b, "p7b")
                vtr += 1
                wt("tensor", "ve", vl)
                use("tensor", tbn)
                tl2 = None
                for k in range(4):
                    tl2 = ins("tensor", lambda e, sems, jj=j, kk=k, b=tb: e.transpose(
                        b[:, kk * 128:(kk + 1) * 128],
                        vtmp[:, jj * 512 + kk * 128: jj * 512 + (kk + 1) * 128],
                        ident[:, :]), inc="te" if k == 3 else None)
                release(f"vtmp{j}", "te")
                wt("vector", "te", tl2)
                dv = vtok[:].rearrange("q (tc c) -> q tc c", c=256)
                t0 = c2 * 32 + r8
                dsty = dv[:, t0:t0 + 25:8, j * 128:(j + 1) * 128]
                srcy = tb[:].rearrange("q (k c) -> q k c", k=8)
                ins("vector", lambda e, sems, d=dsty, s=srcy: e.tensor_copy(
                    d, s[:, 0:4, :]), inc="ve")
                release(tbn, "ve")
            it += 1
            drain_norm(5)
    drain_norm(len(nq))
    qkv_done_ve = C["ve"]
    cm7b.__exit__(None, None, None)
    cm6b.__exit__(None, None, None)
    p6f = stack.enter_context(nc.psum_tensor("p6f", [128, 512], F32))
    p7f = stack.enter_context(nc.psum_tensor("p7f", [128, 512], F32))
    bld.rel["p6f"] = bld.rel.get("p6b")
    bld.rel["p7f"] = bld.rel.get("p7b")

    wt("vector", "ve", nstate["vl"])
    for s8 in range(8):
        vl = ins("vector", lambda e, sems, s=s8: e.reduce_max(
            nmm[0:1, s:s + 1], nmax[0:1, s * 8:(s + 1) * 8], axis=AX.X),
            inc="ve" if s8 == 7 else None)
    wt("vector", "ve", vl)
    v4 = ins("vector", lambda e, sems: e.tensor_mul(
        c2s[0:1, :], nmm[0:1, 0:4], nmm[0:1, 4:8]), inc="ve")
    wt("scalar", "ve", v4)
    s2 = ins("scalar", lambda e, sems: e.activation(cs_[0:1, :], c2s[0:1, :], AF.Sqrt),
             inc="se")
    wt("scalar", "se", s2)
    s3 = ins("scalar", lambda e, sems: e.mul(negc[0:1, :], cs_[0:1, :], -SCALE * 1.02),
             inc="se")
    wt("tensor", "se", s3)
    use("tensor", "p4")
    tl = ins("tensor", lambda e, sems: e.matmul(
        p[4][:, 0:4], ones_row[:, :], negc[0:1, :], start=True, stop=True), inc="te")
    wt("scalar", "te", tl)
    s4 = ins("scalar", lambda e, sems: e.copy(bias_nc[:, :], p[4][:, 0:4]), inc="se")
    release("p4", "se")
    bias_ready = s4

    pt_rel = [None] * 6
    ostage_dma = {}
    ochunk_stq = {}
    first_att = True
    pending_fins = []
    fin_res_all = {}
    o_te_by_par = {0: 0, 1: 0}
    chunk_left = {c: 2 for c in range(8)}
    carry_st = {}
    o_items = []

    def o_slice(slot, dot):
        return o_stage[:, slot * 2048 + dot * 512: slot * 2048 + (dot + 1) * 512]

    def o_src(slot):
        return o_stage[:, slot * 2048:(slot + 1) * 2048]

    def make_opairs(bb, qc, qpar):
        cidx = bb * 4 + qc // 2
        rowc = (qc % 2) * 512
        cast_se = {}
        cast_ve = {}
        t_os = {}

        def em_mm(tnum):
            tsub, dot = tnum // 4, tnum % 4
            ob, on = (p[2], "p2") if tnum % 2 == 0 else (p[4], "p4")
            wt("tensor", "ve", fin_res_all[(qpar, 0)])
            wt("tensor", "ve", fin_res_all[(qpar, 1)])
            use("tensor", on)
            ins("tensor", lambda e, sems, b=ob, t=tsub, d=dot, qp=qpar: e.matmul(
                b[:, :], attnT[:, (qp * 2) * 512 + t * 128:(qp * 2) * 512 + (t + 1) * 128],
                woT_sb[:, d * 512:(d + 1) * 512],
                start=True, stop=False))
            t_o = ins("tensor", lambda e, sems, b=ob, t=tsub, d=dot, qp=qpar: e.matmul(
                b[:, :], attnT[:, (qp * 2 + 1) * 512 + t * 128:(qp * 2 + 1) * 512 + (t + 1) * 128],
                woT_sb[:, 2048 + d * 512: 2048 + (d + 1) * 512],
                start=False, stop=True), inc="te")
            o_te_by_par[qpar] = t_o
            t_os[tnum] = t_o

        def em_cast(tnum):
            tsub, dot = tnum // 4, tnum % 4
            ob, on = (p[2], "p2") if tnum % 2 == 0 else (p[4], "p4")
            slot = tsub % 2
            dsl = o_slice(slot, dot)
            if dot < 2:
                wt("scalar", "te", t_os[tnum])
                if ostage_dma.get(slot) is not None:
                    wt("scalar", "ostq", ostage_dma[slot])
                cast_se[tsub] = ins("scalar", lambda e, sems, d=dsl, b=ob: e.activation(
                    d, b[:, :], AF.Copy), inc="se")
                release(on, "se")
            else:
                wt("vector", "te", t_os[tnum])
                if ostage_dma.get(slot) is not None:
                    wt("vector", "ostq", ostage_dma[slot])
                cast_ve[tsub] = ins("vector", lambda e, sems, d=dsl, b=ob: e.tensor_copy(
                    d, b[:, :]), inc="ve")
                release(on, "ve")
            if dot == 3:
                wt("sync", "se", cast_se[tsub])
                wt("sync", "ve", cast_ve[tsub])
                dst = o_part[cidx][rowc + tsub * 128: rowc + (tsub + 1) * 128, :]
                sd = ins("sync", lambda e, sems, d=dst, s=o_src(slot): e.dma_start(
                    out=d, in_=s), inc="ostq", inc_by=16)
                ostage_dma[slot] = sd
                if tsub == 3:
                    chunk_left[cidx] -= 1
                    if chunk_left[cidx] == 0:
                        ochunk_stq[cidx] = C["ostq"]

        acts = []
        for k in range(16):
            acts.append(lambda n=k: em_mm(n))
            if k >= 1:
                acts.append(lambda n=k - 1: em_cast(n))
        acts.append(lambda: em_cast(15))
        return acts

    _order = [(bb, qc) for bb in range(B) for qc in (7, 6, 5, 4, 3, 2, 1, 0)]
    nxt_block = {a: b for a, b in zip(_order, _order[1:])}
    for bb in range(B):
        for qc in (7, 6, 5, 4, 3, 2, 1, 0):
            nkt = 4 * qc + 4
            qpar = qc % 2
            drain_left = [2 * (nkt - 1)]

            def drain_o(drain_left=drain_left):
                if drain_left[0] <= 0:
                    drain_left[0] -= 1
                    while o_items:
                        o_items.pop(0)()
                    return
                k = -(-len(o_items) // drain_left[0])
                drain_left[0] -= 1
                for _ in range(k):
                    if o_items:
                        o_items.pop(0)()

            for hc in range(HPC):
                bidx = bb * 2 + hc
                pvb, pvn = (p[5], "p5") if hc == 0 else (p6f, "p6f")
                dnb, dnn = (p[3], "p3") if hc == 0 else (p7f, "p7f")
                kslice = lambda k, h=hc, b=bb: kt_[:, h * TALL + b * SEQ + k * 128:
                                                   h * TALL + b * SEQ + (k + 1) * 128]
                qslice = qt[:, hc * TALL + bb * SEQ + qc * 512:
                            hc * TALL + bb * SEQ + (qc + 1) * 512]

                def emit_st(kti, hc=hc, qslice=qslice, kslice=kslice):
                    stb, stn = ((p[0], "p0"), (p[1], "p1"))[kti % 2]
                    use("tensor", stn)
                    return ins("tensor", lambda e, sems, b=stb, k=kslice(kti), q=qslice:
                               e.matmul(b[:, :], k, q, start=True, stop=True), inc="te")

                if first_att:
                    wt("tensor", "ve", qkv_done_ve)
                    wt("tensor", "se", bias_ready)
                    wt("tensor", "gpq", wo_ld)
                    first_att = False
                if hc == 0 and carry_st:
                    t_st = carry_st
                    carry_st = {}
                else:
                    t_st = {}
                    for kk in range(min(2, nkt)):
                        t_st[kk] = emit_st(kk)
                for kti in range(nkt):
                    stb, stn = ((p[0], "p0"), (p[1], "p1"))[kti % 2]
                    wt("scalar", "te", t_st[kti])
                    bld.we("scalar", pt_rel[kti % 6])
                    ex = ins("scalar", lambda e, sems, b=stb, sl=kti % 6, bi=bidx: e.activation(
                        pt[:, sl * 512:(sl + 1) * 512], b[:, :], AF.Exp,
                        bias=bias_nc[:, bi:bi + 1], scale=SCALE), inc="se")
                    release(stn, "se")
                    if kti >= 4 * qc:
                        a = kti - 4 * qc
                        wt("vector", "gpq", c_consts)
                        wt("vector", "se", ex)
                        mk = ins("vector", lambda e, sems, sl=kti % 6, aa=a: e.tensor_mul(
                            pt[:, sl * 512:(sl + 1) * 512],
                            pt[:, sl * 512:(sl + 1) * 512],
                            masks[:, aa * 512:(aa + 1) * 512]), inc="ve")
                        pt_ready = ("ve", mk)
                    else:
                        pt_ready = ("se", ex)
                    bld.we("tensor", pt_ready)
                    if kti == 0:
                        use("tensor", dnn)
                        use("tensor", pvn)
                    ins("tensor", lambda e, sems, sl=kti % 6, k0=(kti == 0), b=dnb,
                        kl=(kti == nkt - 1): e.matmul(
                        b[:, :], sel32[:, :], pt[:, sl * 512:(sl + 1) * 512],
                        start=k0, stop=kl))
                    vsl = vtok[:, (bb * 32 + kti) * 256 + hc * 128:
                               (bb * 32 + kti) * 256 + (hc + 1) * 128]
                    t_pv = ins("tensor", lambda e, sems, sl=kti % 6, v=vsl, k0=(kti == 0), b=pvb,
                               kl=(kti == nkt - 1): e.matmul(
                        b[:, :], v, pt[:, sl * 512:(sl + 1) * 512],
                        start=k0, stop=kl), inc="te")
                    pt_rel[kti % 6] = ("te", t_pv)
                    if kti + 2 < nkt:
                        t_st[kti + 2] = emit_st(kti + 2)
                    if kti == 0:
                        while pending_fins:
                            pending_fins.pop(0)()
                    else:
                        drain_o()
                wt("scalar", "te", t_pv)
                use("scalar", "recip")
                s_ln = ins("scalar", lambda e, sems, b=dnb: e.activation(
                    recip[0:1, :], b[0:1, :], AF.Ln), inc="se")
                release(dnn, "se")

                def make_fin(s_ln=s_ln, hc=hc, qp=qpar, pvb=pvb, pvn=pvn,
                             dnb=dnb, dnn=dnn, t_pv=t_pv):
                    def fin():
                        wt("tensor", "se", s_ln)
                        use("tensor", dnn)
                        t_rb = ins("tensor", lambda e, sems: e.matmul(
                            dnb[:, :], ones_row[:, :], recip[0:1, :],
                            start=True, stop=True), inc="te")
                        release("recip", "te")
                        wt("scalar", "te", t_rb)
                        use("scalar", "rb_sb")
                        s_rb = ins("scalar", lambda e, sems: e.activation(
                            rb_sb[:, :], dnb[:, :], AF.Exp, scale=-1.0), inc="se")
                        release(dnn, "se")
                        wt("vector", "se", s_rb)
                        wt("vector", "te", t_pv)
                        wt("vector", "te", o_te_by_par[qp])
                        va = ins("vector", lambda e, sems: e.tensor_mul(
                            attnT[:, (qp * 2 + hc) * 512:(qp * 2 + hc + 1) * 512],
                            pvb[:, :], rb_sb[:, :]), inc="ve")
                        release(pvn, "ve")
                        release("rb_sb", "ve")
                        fin_res_all[(qp, hc)] = va
                    return fin
                pending_fins.append(make_fin())
            nxt = nxt_block.get((bb, qc))
            pre_st = {}
            if nxt is not None:
                nbb, nqc = nxt
                nk = lambda k, b=nbb: kt_[:, 0 * TALL + b * SEQ + k * 128:
                                          0 * TALL + b * SEQ + (k + 1) * 128]
                nq = qt[:, 0 * TALL + nbb * SEQ + nqc * 512:
                        0 * TALL + nbb * SEQ + (nqc + 1) * 512]
                for kk in range(min(2, 4 * nqc + 4)):
                    stb, stn = ((p[0], "p0"), (p[1], "p1"))[kk % 2]
                    use("tensor", stn)
                    pre_st[kk] = ins("tensor", lambda e, sems, b=stb, k=nk(kk), q=nq:
                                     e.matmul(b[:, :], k, q, start=True, stop=True),
                                     inc="te")
            carry_st = pre_st
            while o_items:
                o_items.pop(0)()
            o_items = make_opairs(bb, qc, qpar)
    while pending_fins:
        pending_fins.pop(0)()
    while o_items:
        o_items.pop(0)()

    chunk_order = [bb * 4 + k for bb in range(B) for k in (3, 2, 1, 0)]
    for n, cidx in enumerate(chunk_order):
        wt("gpsimd", "ostq", ochunk_stq[cidx])
        if n > 0:
            wt("gpsimd", f"cc{1 + chunk_order[n - 1]}", 1)
        ins("gpsimd", lambda e, sems, c=cidx: e.collective_compute(
            "ReduceScatter", ALU.add, [list(range(NCORES))],
            ins=[o_part[c]], outs=[rs_dst[c]]), inc=f"cc{1 + cidx}")
    for n, cidx in enumerate(chunk_order):
        wt("sync", f"cc{1 + cidx}", 1)
        use("sync", f"rs_in{n % 2}")
        ins("sync", lambda e, sems, c=cidx, jj=n % 2: e.dma_start(
            out=rs_in[:, jj * 2048:(jj + 1) * 2048], in_=rs_dst[c]),
            inc="ldq", inc_by=16)
        use("sync", f"x_ep{n % 2}")
        l2 = ins("sync", lambda e, sems, c=cidx, jj=n % 2: e.dma_start(
            out=x_ep[jj][:, :],
            in_=x_own[c * 128:(c + 1) * 128, :]),
            inc="ldq", inc_by=16)
        wt("gpsimd", f"cc{1 + cidx}", 1)
        wt("gpsimd", "ldq", l2)
        jj = n % 2
        use("gpsimd", f"oout{jj}")
        g1 = ins("gpsimd", lambda e, sems, c=cidx, j=jj: e.tensor_add(
            oout[j][:, :], rs_in[:, j * 2048:(j + 1) * 2048],
            ffn_tok[:, c * 2048:(c + 1) * 2048]), inc="gp")
        wt("gpsimd", "gp", g1)
        g2 = ins("gpsimd", lambda e, sems, j=jj: e.tensor_add(
            oout[j][:, :], oout[j][:, :], x_ep[j][:, :]), inc="gp")
        release(f"x_ep{jj}", "gp")
        release(f"rs_in{jj}", "gp")
        wt("scalar", "gp", g2)
        ins("scalar", lambda e, sems, c=cidx, j=jj: e.dma_start(
            out=out_parts[c, :, :], in_=oout[j][:, :]),
            inc="stq", inc_by=16)
        release(f"oout{jj}", "stq")
    if debug:
        for sname in ("te", "se", "ve", "gp"):
            wt("scalar", sname, C[sname])
        for i in range(9):
            wt("scalar", f"cc{i}", 1)
        wt("scalar", "stq", C["stq"])
        wt("scalar", "ldq", C["ldq"])
        taps = [
            ("dbg_xn_src", xn_src, [D, TOWN], BF16, "dram"),
            ("dbg_qt", qt, None, BF16, "sbuf"),
            ("dbg_kt", kt_, None, BF16, "sbuf"),
            ("dbg_vtok", vtok, None, BF16, "sbuf"),
            ("dbg_ffn", ffn_tok, None, BF16, "sbuf"),
            ("dbg_bias", bias_nc, None, F32, "sbuf"),
            ("dbg_nmm", nmm, None, F32, "sbuf"),
        ] + [(f"dbg_opart{c}", o_part[c], [1024, D], BF16, "dram") for c in range(8)] \
          + [(f"dbg_rs{c}", rs_dst[c], [128, D], BF16, "dram") for c in range(8)]
        for name, srcb, shp, dty, kind in taps:
            if kind == "dram":
                dst = nc.dram_tensor(name, shp, dty, kind="ExternalOutput").ap()
                ins("scalar", lambda e, sems, d=dst, s=srcb: e.dma_start(out=d, in_=s),
                    inc="stq", inc_by=16)
            else:
                shp2 = list(srcb.shape)
                dst = nc.dram_tensor(name, shp2, dty, kind="ExternalOutput").ap()
                ins("scalar", lambda e, sems, d=dst, s=srcb: e.dma_start(out=d, in_=s[:, :]),
                    inc="stq", inc_by=16)
    wt("scalar", "stq", C["stq"])
    wt("sync", "stq", C["stq"])
    wt("sync", "ostq", C["ostq"])

    with contextlib.ExitStack() as es:
        sems = {}
        for s in bld.semnames:
            sems[s] = es.enter_context(nc.semaphore(s))
        with nc.Block() as block:
            @block.sync
            def _(e):
                for fn in bld.streams["sync"]:
                    fn(e, sems)

            @block.scalar
            def _(e):
                for fn in bld.streams["scalar"]:
                    fn(e, sems)

            @block.vector
            def _(e):
                for fn in bld.streams["vector"]:
                    fn(e, sems)

            @block.tensor
            def _(e):
                for fn in bld.streams["tensor"]:
                    fn(e, sems)

            @block.gpsimd
            def _(e):
                for fn in bld.streams["gpsimd"]:
                    fn(e, sems)
    stack.close()
    return nc


def make_inputs(L0_attn_x, L0_attn_Wq, L0_attn_Wk, L0_attn_Wv, L0_attn_Wo,
                L0_ffn_gate_w, L0_ffn_up_w, L0_ffn_down_w,
                L0_ffn_gate_s, L0_ffn_up_s, L0_ffn_down_s):
    bf = ml_dtypes.bfloat16
    x = np.ascontiguousarray(np.asarray(L0_attn_x, dtype=np.float32).reshape(TALL, D))
    x4 = x.reshape(8, NCORES, 128, D)

    masks = np.zeros((128, 4 * 512), dtype=bf)
    for a in range(4):
        pp = np.arange(128)[:, None]
        jj = np.arange(512)[None, :]
        masks[:, a * 512:(a + 1) * 512] = (a * 128 + pp <= jj).astype(bf)
    ident = np.eye(128, dtype=bf)
    ones_col = np.ones((128, 1), dtype=bf)
    ones_row = np.ones((1, 128), dtype=np.float32)
    sel32 = np.zeros((128, 128), dtype=bf)
    sel32[:, 0] = 1.0
    ones_row_bf = ones_row.astype(bf)
    gs = np.ascontiguousarray(np.asarray(L0_ffn_gate_s, np.float32).reshape(NIC, 128).T)
    us = np.ascontiguousarray(np.asarray(L0_ffn_up_s, np.float32).reshape(NIC, 128).T)
    ds = np.ascontiguousarray(np.asarray(L0_ffn_down_s, np.float32).reshape(NDC, 128).T)
    gT = np.ascontiguousarray(np.asarray(L0_ffn_gate_w, np.float32)
                              .reshape(NIC, 128, NDC, 128).transpose(0, 3, 2, 1)
                              .reshape(NIC * 128, D))
    uT = np.ascontiguousarray(np.asarray(L0_ffn_up_w, np.float32)
                              .reshape(NIC, 128, NDC, 128).transpose(0, 3, 2, 1)
                              .reshape(NIC * 128, D))
    dT = np.ascontiguousarray(np.asarray(L0_ffn_down_w, np.float32)
                              .reshape(NDC, 128, NIC, 128).transpose(0, 3, 2, 1)
                              .reshape(NDC * 128, INTER))
    Wq = np.asarray(L0_attn_Wq, np.float32)
    Wk = np.asarray(L0_attn_Wk, np.float32)
    Wv = np.asarray(L0_attn_Wv, np.float32)
    WoT = np.ascontiguousarray(np.asarray(L0_attn_Wo, np.float32).T)

    def wtile(W, r):
        wo = W[2 * r * DH:(2 * r + 2) * DH, :]
        return np.ascontiguousarray(
            wo.reshape(HPC * DH, NDC, 128).transpose(2, 1, 0).reshape(128, NDC * 256))

    in_maps = []
    for r in range(NCORES):
        sl = slice(2 * r * DH, (2 * r + 2) * DH)
        in_maps.append({
            "x_own": np.ascontiguousarray(x4[:, r].reshape(TOWN, D)),
            "wqT": wtile(Wq, r),
            "wkT": wtile(Wk, r),
            "wvT": wtile(Wv, r),
            "woT": np.ascontiguousarray(WoT[sl, :]),
            "gateT": gT, "upT": uT, "downT": dT,
            "gs": gs, "us": us, "ds": ds,
            "masks": masks, "ident": ident,
            "ones_col": ones_col, "ones_row": ones_row, "sel32": sel32,
            "ones_row_bf": ones_row_bf,
        })
    return in_maps


def kernel(L0_attn_x, L0_attn_Wq, L0_attn_Wk, L0_attn_Wv, L0_attn_Wo,
           L0_ffn_gate_w, L0_ffn_up_w, L0_ffn_down_w,
           L0_ffn_gate_s, L0_ffn_up_s, L0_ffn_down_s):
    in_maps = make_inputs(L0_attn_x, L0_attn_Wq, L0_attn_Wk, L0_attn_Wv, L0_attn_Wo,
                          L0_ffn_gate_w, L0_ffn_up_w, L0_ffn_down_w,
                          L0_ffn_gate_s, L0_ffn_up_s, L0_ffn_down_s)
    nc = build_nc()
    res = run_bass_kernel_spmd(nc, in_maps, core_ids=list(range(NCORES)))
    out = np.empty((8, NCORES, 128, D), dtype=np.float32)
    for r in range(NCORES):
        out[:, r] = res.results[r]["out_parts"]
    return out.reshape(B, SEQ, D)


# version 51
# speedup vs baseline: 1.0079x; 1.0079x over previous
import contextlib
import math
import numpy as np
import ml_dtypes
import concourse.bass as bass
import concourse.mybir as mybir
from concourse.bass_utils import run_bass_kernel_spmd

F32 = mybir.dt.float32
BF16 = mybir.dt.bfloat16
AF = mybir.ActivationFunctionType
ALU = mybir.AluOpType
AX = mybir.AxisListType

B = 2
SEQ = 4096
D = 2048
DH = 128
HPC = 2
INTER = 5632
NIC = INTER // 128
NDC = D // 128
TOWN = 1024
TALL = 8192
NCORES = 8
SCALE = 1.0 / math.sqrt(DH)
EPS = 1e-5

ENGS = ("sync", "scalar", "vector", "tensor", "gpsimd")


class Bld:

    def __init__(self, nc):
        self.nc = nc
        self.streams = {e: [] for e in ENGS}
        self.C = {}
        self.wmax = {e: {} for e in ENGS}
        self.semnames = []
        self.rel = {}

    def sem(self, name):
        self.C[name] = 0
        self.semnames.append(name)

    def wt(self, eng, sname, cnt):
        if cnt is None or cnt <= 0:
            return
        hw = self.wmax[eng].get(sname, -1)
        if cnt <= hw:
            return
        self.wmax[eng][sname] = cnt

        def fn(e, sems, s=sname, c=cnt):
            e.wait_ge(sems[s], c)
        self.streams[eng].append(fn)

    def we(self, eng, ev):
        if ev is not None:
            self.wt(eng, ev[0], ev[1])

    def use(self, eng, res):
        self.we(eng, self.rel.get(res))

    def release(self, res, sname):
        self.rel[res] = (sname, self.C[sname])

    def ins(self, eng, fn, inc=None, inc_by=1):
        out = None
        if inc is not None:
            self.C[inc] += inc_by
            out = self.C[inc]

            def wrapped(e, sems, fn=fn, s=inc, n=inc_by):
                fn(e, sems).then_inc(sems[s], n)
            self.streams[eng].append(wrapped)
        else:
            def wrapped2(e, sems, fn=fn):
                fn(e, sems)
            self.streams[eng].append(wrapped2)
        return out


def build_nc(debug=False):
    nc = bass.Bass("TRN2", target_bir_lowering=False)
    bld = Bld(nc)
    ins, wt, use, release = bld.ins, bld.wt, bld.use, bld.release
    C = bld.C

    def din(name, shape, dtype):
        return nc.dram_tensor(name, shape, dtype, kind="ExternalInput").ap()

    x_own = din("x_own", [TOWN, D], F32)
    wqT = din("wqT", [128, NDC * 256], F32)
    wkT = din("wkT", [128, NDC * 256], F32)
    wvT = din("wvT", [128, NDC * 256], F32)
    woT_in = din("woT", [HPC * DH, D], F32)
    gateT = din("gateT", [NIC * 128, D], F32)
    upT = din("upT", [NIC * 128, D], F32)
    downT = din("downT", [NDC * 128, INTER], F32)
    gs_in = din("gs", [128, NIC], F32)
    us_in = din("us", [128, NIC], F32)
    ds_in = din("ds", [128, NDC], F32)
    masks_in = din("masks", [128, 4 * 512], BF16)
    ident_in = din("ident", [128, 128], BF16)
    onesc_in = din("ones_col", [128, 1], BF16)
    onesr_in = din("ones_row", [1, 128], F32)
    onesrb_in = din("ones_row_bf", [1, 128], BF16)
    sel32_in = din("sel32", [128, 128], BF16)

    out_parts = nc.dram_tensor("out_parts", [8, 128, D], F32, kind="ExternalOutput").ap()

    xn_src = nc.dram_tensor("xn_src", [D, TOWN], BF16, kind="Internal").ap()
    xn_all = nc.dram_tensor("xn_all", [NCORES * D, TOWN], BF16, kind="Internal",
                            addr_space="Shared").ap()
    o_part = [nc.dram_tensor(f"o_part{c}", [1024, D], BF16, kind="Internal").ap()
              for c in range(8)]
    rs_dst = [nc.dram_tensor(f"rs_dst{c}", [128, D], BF16, kind="Internal").ap()
              for c in range(8)]
    bar_src = nc.dram_tensor("bar_src", [1, 64], F32, kind="Internal").ap()
    bar_dst = [nc.dram_tensor(f"bar_dst{i}", [1, 64], F32, kind="Internal").ap()
               for i in range(6)]

    SB_BASE = 16512

    def sb(name, shape, dtype, off):
        return nc.alloc_sbuf_tensor_at(name, shape, dtype, offset=SB_BASE + off)

    masks = sb("masks", [128, 4 * 512], BF16, 0)
    ident = sb("ident", [128, 128], BF16, 4096)
    ones_col = sb("ones_col", [128, 1], BF16, 4352)
    ones_row = sb("ones_row", [1, 128], F32, 4384)
    gs = sb("gs", [128, NIC], F32, 4896)
    us = sb("us", [128, NIC], F32, 5088)
    ds = sb("ds", [128, NDC], F32, 5280)
    bias_nc = sb("bias_nc", [128, 4], F32, 5344)
    stats6 = sb("stats6", [128, 24], F32, 5408)
    mv2 = sb("mv2", [128, 16], F32, 5504)
    meanc = sb("meanc", [128, 8], F32, 5568)
    varc = sb("varc", [128, 8], F32, 5600)
    stdc = sb("stdc", [128, 8], F32, 5632)
    rstdc = sb("rstdc", [128, 8], F32, 5664)
    nmrc = sb("nmrc", [128, 8], F32, 5696)
    eps_col = sb("eps_col", [128, 1], F32, 6112)
    nmax = sb("nmax", [1, 64], F32, 5728)
    nmm = sb("nmm", [1, 8], F32, 5984)
    c2s = sb("c2s", [1, 4], F32, 6016)
    cs_ = sb("cs_", [1, 4], F32, 6048)
    negc = sb("negc", [1, 4], F32, 6080)
    ffn_tok = sb("ffn_tok", [128, 8 * D], BF16, 8000)
    woT_sb = sb("woT_sb", [128, HPC * D], BF16, 40768)
    sel32 = sb("sel32", [128, 128], BF16, 6144)
    ones_row_bf = sb("ones_row_bf", [1, 128], BF16, 6400)
    recip_bf = sb("recip_bf", [1, 512], BF16, 6656)
    pt = sb("pt", [128, 7 * 512], BF16, 48960)
    attnT = sb("attnT", [128, 4 * 512], BF16, 56128)
    recip = sb("recip", [1, 512], F32, 60224)
    rb_sb = sb("rb_sb", [128, 512], F32, 62272)
    o_stage = sb("o_stage", [128, 2 * D], BF16, 64320)
    dtmp = sb("dtmp", [128, 2 * 512], BF16, 48960)
    sgt = sb("sgt", [128, 2 * 512], BF16, 51008)
    vtmp = sb("vtmp", [128, 2 * 512], BF16, 48960)
    dstrip2 = sb("dstrip2", [128, NIC * 128], BF16, 53056)

    xt = [sb(f"xt{i}", [128, D], F32, 72512 + i * 8192) for i in range(2)]
    xn_tok = [sb(f"xn_tok{i}", [128, D], BF16, 88896 + i * 4096) for i in range(2)]
    xnT = sb("xnT", [128, NDC * TOWN], BF16, 180064)
    h_all = sb("h_all", [128, NIC * 512], BF16, 72512)
    gstrip = [sb(f"gstrip{i}", [128, NDC * 128], BF16, 117568 + i * 4096) for i in range(4)]
    ustrip = [sb(f"ustrip{i}", [128, NDC * 128], BF16, 133952 + i * 4096) for i in range(4)]
    dstrip = [sb(f"dstrip{i}", [128, NIC * 128], BF16, 150336 + i * 11264) for i in range(2)]
    dstrip.append(dstrip2)
    dstrip.append(sb("dstrip3", [128, NIC * 128], BF16, 117568))
    dstrip.append(sb("dstrip4", [128, NIC * 128], BF16, 133952))
    xn_in = [sb(f"xn_in{i}", [128, 8 * 512], BF16, 72512 + i * 8192) for i in range(3)]
    qt = sb("qt", [128, HPC * TALL], BF16, 97088)
    kt_ = sb("kt", [128, HPC * TALL], BF16, 129856)
    vtok = sb("vtok", [128, 64 * 256], BF16, 162624)
    wq_sb = sb("wqsb", [128, NDC * 256], BF16, 180064)
    wk_sb = sb("wksb", [128, NDC * 256], BF16, 188256)
    wv_sb = sb("wvsb", [128, NDC * 256], BF16, 196448)
    sq_sb = sb("sq_sb", [128, 4096], BF16, 51008)
    rs_in = sb("rs_in", [128, 2 * D], BF16, 72512)
    x_ep = [sb(f"x_ep{i}", [128, D], F32, 195392 + i * 8192) for i in range(2)]
    oout = [sb(f"oout{i}", [128, D], F32, 80704 + i * 8192) for i in range(2)]

    stack = contextlib.ExitStack()
    p = [stack.enter_context(nc.psum_tensor(f"p{i}", [128, 512], F32)) for i in range(6)]
    cm6 = nc.psum_tensor("p6", [128, 1024], BF16)
    cm7 = nc.psum_tensor("p7", [128, 1024], BF16)
    p6 = cm6.__enter__()
    p7 = cm7.__enter__()

    for s in ("ldq", "gpq", "stq", "ostq", "te", "se", "ve", "gp",
              "cc0", "cc1", "cc2", "cc3", "cc4", "cc5", "cc6", "cc7", "cc8"):
        bld.sem(s)

    def gdma(dst_ap, src_ap):
        return ins("gpsimd", lambda e, sems, d=dst_ap, s=src_ap: e.dma_start(out=d, in_=s),
                   inc="gpq", inc_by=16)

    nbar = [0]

    def barrier():
        i = nbar[0]
        nbar[0] += 1
        ins("gpsimd", lambda e, sems, ii=i: e.collective_compute(
            "AllReduce", ALU.add, [list(range(NCORES))],
            ins=[bar_src], outs=[bar_dst[ii]]), inc="gp")
        wt("gpsimd", "gp", C["gp"])

    gdma(bar_src, onesr_in[0:1, 0:64])
    gdma(sel32[:, :], sel32_in)
    gdma(ones_row_bf[:, :], onesrb_in)
    gdma(masks[:, :], masks_in)
    gdma(ident[:, :], ident_in)
    gdma(ones_col[:, :], onesc_in)
    gdma(ones_row[:, :], onesr_in)
    gdma(gs[:, :], gs_in)
    gdma(us[:, :], us_in)
    gdma(ds[:, :], ds_in)
    c_consts = C["gpq"]
    wt("gpsimd", "gpq", c_consts)
    barrier()

    v1b = ins("vector", lambda e, sems: e.memset(eps_col[:, :], EPS), inc="ve")
    ev = None
    for tc in range(8):
        use("sync", f"xt{tc % 2}")
        ld = ins("sync", lambda e, sems, t=tc: e.dma_start(
            out=xt[t % 2][:, :], in_=x_own[t * 128:(t + 1) * 128, :]),
            inc="ldq", inc_by=16)
        wt("vector", "ldq", ld)
        last = None
        for c6 in range(4):
            last = ins("vector", lambda e, sems, t=tc, c=c6: e.bn_stats(
                stats6[:, c * 6:(c + 1) * 6], xt[t % 2][:, c * 512:(c + 1) * 512]),
                inc="ve" if c6 == 3 else None)
        wt("vector", "ve", last)
        va = ins("vector", lambda e, sems, t=tc: e.bn_aggr(
            mv2[:, t * 2:(t + 1) * 2], stats6[:, :]), inc="ve")
        wt("scalar", "ve", va)
        s1 = ins("scalar", lambda e, sems, t=tc: e.activation(
            stdc[:, t:t + 1], mv2[:, 2 * t + 1:2 * t + 2], AF.Sqrt,
            bias=eps_col[:, 0:1]), inc="se")
        wt("vector", "se", s1)
        v2 = ins("vector", lambda e, sems, t=tc: e.reciprocal(
            rstdc[:, t:t + 1], stdc[:, t:t + 1]), inc="ve")
        wt("vector", "ve", v2)
        v3 = ins("vector", lambda e, sems, t=tc: e.scalar_tensor_tensor(
            nmrc[:, t:t + 1], mv2[:, 2 * t:2 * t + 1], -1.0, rstdc[:, t:t + 1],
            ALU.mult, ALU.mult), inc="ve")
        wt("vector", "ve", v3)
        use("vector", f"xn_tok{tc % 2}")
        se_ = ins("vector", lambda e, sems, t=tc: e.tensor_scalar(
            xn_tok[t % 2][:, :], xt[t % 2][:, :], rstdc[:, t:t + 1],
            nmrc[:, t:t + 1], ALU.mult, ALU.add), inc="ve")
        release(f"xt{tc % 2}", "ve")
        for half, bank, bname in ((0, p6, "p6"), (1, p7, "p7")):
            wt("tensor", "gpq", c_consts)
            wt("tensor", "ve", se_)
            use("tensor", bname)
            tl = None
            for k in range(8):
                dc = half * 8 + k
                tl = ins("tensor", lambda e, sems, t=tc, d=dc, b=bank, kk=k: e.transpose(
                    b[:, kk * 128:(kk + 1) * 128], xn_tok[t % 2][:, d * 128:(d + 1) * 128],
                    ident[:, :]), inc="te" if k == 7 else None)
            wt("vector", "te", tl)
            dst = xnT[:].rearrange("q (dc t) -> q dc t", dc=NDC)[
                :, half * 8:(half + 1) * 8, tc * 128:(tc + 1) * 128]
            src = bank[:].rearrange("q (k c) -> q k c", k=8)
            ev = ins("vector", lambda e, sems, d=dst, s=src: e.tensor_copy(d, s), inc="ve")
            release(bname, "ve")
        release(f"xn_tok{tc % 2}", "ve")
    xnT_done = ("ve", ev)
    bld.we("scalar", xnT_done)
    xnsrc_ap = xn_src.rearrange("(dc q) t -> q dc t", q=128)
    st1 = ins("scalar", lambda e, sems: e.dma_start(
        out=xnsrc_ap, in_=xnT[:].rearrange("q (dc t) -> q dc t", dc=NDC)),
        inc="stq", inc_by=16)

    def strip_dma(kind, ic, slot):
        if kind == "g":
            src = gateT[ic * 128:(ic + 1) * 128, :]
            dst = gstrip[slot][:, :]
        elif kind == "u":
            src = upT[ic * 128:(ic + 1) * 128, :]
            dst = ustrip[slot][:, :]
        else:
            src = downT[ic * 128:(ic + 1) * 128, :]
            dst = dstrip[slot][:, :]
        use("gpsimd", f"{kind}slot{slot}")
        return ins("gpsimd", lambda e, sems, s=src, d=dst: e.dma_start(out=d, in_=s),
                   inc="gpq", inc_by=16)

    gu_lds = {}
    for ic in range(4):
        gu_lds[("g", ic)] = strip_dma("g", ic, ic % 4)
        gu_lds[("u", ic)] = strip_dma("u", ic, ic % 4)

    te_down_last = 0
    te_after_gu = 0
    for th in range(2):
        for ic in range(NIC):
            kg, ku = ("g", ic if th == 0 else 100 + ic), ("u", ic if th == 0 else 100 + ic)
            if kg not in gu_lds:
                gu_lds[kg] = strip_dma("g", ic, ic % 4)
                gu_lds[ku] = strip_dma("u", ic, ic % 4)
            gb, ub = (p[0], p[1]) if ic % 2 == 0 else (p[2], p[3])
            gbn, ubn = ("p0", "p1") if ic % 2 == 0 else ("p2", "p3")
            wt("tensor", "gpq", gu_lds[kg])
            wt("tensor", "gpq", gu_lds[ku])
            if th == 0 and ic == 0:
                bld.we("tensor", xnT_done)
            use("tensor", gbn)
            use("tensor", ubn)
            tl = None
            for dc in range(NDC):
                rhs = xnT[:, dc * TOWN + th * 512: dc * TOWN + th * 512 + 512]
                ins("tensor", lambda e, sems, d=dc, g=gb, i=ic, r=rhs: e.matmul(
                    g[:, :], gstrip[i % 4][:, d * 128:(d + 1) * 128], r,
                    start=(d == 0), stop=(d == NDC - 1)))
                tl = ins("tensor", lambda e, sems, d=dc, u=ub, i=ic, r=rhs: e.matmul(
                    u[:, :], ustrip[i % 4][:, d * 128:(d + 1) * 128], r,
                    start=(d == 0), stop=(d == NDC - 1)),
                    inc="te" if dc == NDC - 1 else None)
            release(f"gslot{ic % 4}", "te")
            release(f"uslot{ic % 4}", "te")
            wt("scalar", "gpq", c_consts)
            wt("scalar", "te", tl)
            use("scalar", f"sgt{ic % 2}")
            sl = ins("scalar", lambda e, sems, g=gb, i=ic: e.activation(
                sgt[:, (i % 2) * 512:(i % 2) * 512 + 512], g[:, :], AF.Silu,
                scale=gs[:, i:i + 1]), inc="se")
            release(gbn, "se")
            wt("vector", "se", sl)
            wt("vector", "te", tl)
            if th == 1 and ic == 0:
                wt("vector", "te", te_down_last)
            vl = ins("vector", lambda e, sems, u=ub, i=ic: e.scalar_tensor_tensor(
                h_all[:, i * 512:(i + 1) * 512], u[:, :], us[:, i:i + 1],
                sgt[:, (i % 2) * 512:(i % 2) * 512 + 512], ALU.mult, ALU.mult), inc="ve")
            release(ubn, "ve")
            release(f"sgt{ic % 2}", "ve")
        h_done = C["ve"]
        te_after_gu = C["te"]
        if th == 0:
            d_pre = {}
            bld.rel["dslot3"] = ("te", C["te"])
            bld.rel["dslot4"] = ("te", C["te"])
            for do in range(5):
                d_pre[do] = strip_dma("d", do, do)
            wt("gpsimd", "gpq", C["gpq"])
            wt("gpsimd", "stq", st1)
            barrier()
            ins("gpsimd", lambda e, sems: e.collective_compute(
                "AllGather", ALU.bypass, [list(range(NCORES))],
                ins=[xn_src], outs=[xn_all]), inc="cc0")
            wt("gpsimd", "cc0", 1)
        if debug:
            dbgh = nc.dram_tensor(f"dbg_h{th}", [128, NIC * 512], BF16,
                                  kind="ExternalOutput").ap()
            wt("scalar", "ve", h_done)
            hd = ins("scalar", lambda e, sems, d=dbgh: e.dma_start(out=d, in_=h_all[:, :]),
                     inc="stq", inc_by=16)
            wt("vector", "stq", hd)
        d_lds = dict(d_pre) if th == 0 else {}

        def dslot_of(i, th=th):
            if th == 0:
                return i if i < 5 else (i - 5) % 3
            return i % 3
        for do in range(NDC):
            while len(d_lds) < min(do + 3, NDC):
                nxt = len(d_lds)
                d_lds[nxt] = strip_dma("d", nxt, dslot_of(nxt))
            bank, bn_ = (p[0], "p0") if do % 2 == 0 else (p[1], "p1")
            wt("tensor", "gpq", d_lds[do])
            wt("tensor", "ve", h_done)
            use("tensor", bn_)
            tl = None
            for kc in range(NIC):
                tl = ins("tensor", lambda e, sems, k=kc, b=bank, d=dslot_of(do): e.matmul(
                    b[:, :], dstrip[d][:, k * 128:(k + 1) * 128],
                    h_all[:, k * 512:(k + 1) * 512],
                    start=(k == 0), stop=(k == NIC - 1)),
                    inc="te" if kc == NIC - 1 else None)
            release(f"dslot{dslot_of(do)}", "te")
            if th == 0 and do == 3:
                for s_ in ("gslot0", "gslot1", "gslot2"):
                    release(s_, "te")
            if th == 0 and do == 4:
                for s_ in ("uslot0", "uslot1", "uslot2"):
                    release(s_, "te")
            te_down_last = tl
            wt("scalar", "te", tl)
            use("scalar", f"dtmp{do % 2}")
            sl = ins("scalar", lambda e, sems, b=bank, d=do: e.activation(
                dtmp[:, (d % 2) * 512:(d % 2) * 512 + 512], b[:, :], AF.Copy,
                scale=ds[:, d:d + 1]), inc="se")
            release(bn_, "se")
            tb, tbn = (p6, "p6") if do % 2 == 0 else (p7, "p7")
            wt("tensor", "se", sl)
            use("tensor", tbn)
            tl2 = None
            for j in range(4):
                tl2 = ins("tensor", lambda e, sems, d=do, jj=j, b=tb: e.transpose(
                    b[:, jj * 128:(jj + 1) * 128],
                    dtmp[:, (d % 2) * 512 + jj * 128:(d % 2) * 512 + (jj + 1) * 128],
                    ident[:, :]), inc="te" if j == 3 else None)
            wt("vector", "te", tl2)
            dstv = ffn_tok[:].rearrange("q (t c) -> q t c", t=8)[
                :, th * 4:(th + 1) * 4, do * 128:(do + 1) * 128]
            srcv = tb[:].rearrange("q (j c) -> q j c", j=8)[:, 0:4, :]
            ins("vector", lambda e, sems, d=dstv, s=srcv: e.tensor_copy(d, s), inc="ve")
            release(tbn, "ve")
            release(f"dtmp{do % 2}", "ve")
    te_after_ffn = C["te"]

    wt("gpsimd", "cc0", 1)
    wt("gpsimd", "te", te_after_gu)
    wq_ld = gdma(wq_sb[:, :], wqT)
    wk_ld = gdma(wk_sb[:, :], wkT)
    wv_ld = gdma(wv_sb[:, :], wvT)
    wo_ld = gdma(woT_sb[:].rearrange("q (hc c) -> q hc c", hc=HPC),
                 woT_in.rearrange("(hc q) c -> q hc c", q=128))

    cm7.__exit__(None, None, None)
    cm6.__exit__(None, None, None)
    cm6f = nc.psum_tensor("p6fa", [128, 512], F32)
    cm7f = nc.psum_tensor("p7fa", [128, 512], F32)
    p6f = cm6f.__enter__()
    p7f = cm7f.__enter__()
    bld.rel["p6f"] = bld.rel.get("p6")
    bld.rel["p7f"] = bld.rel.get("p7")

    xn_all_v = xn_all.rearrange("(r dc q) t -> r q dc t", r=NCORES, q=128)
    it = 0
    hbuf = 0
    nq = []
    nstate = {"te_prev": 0, "sqd": {}, "vl": None}

    def drain_norm(k):
        for _ in range(k):
            if not nq:
                return
            kind, s8, arg = nq.pop(0)
            qk, nbb, nhc = s8 // 4, (s8 % 4) // 2, s8 % 2
            if kind == "sq":
                srcn = (qt if qk == 0 else kt_)[:, nhc * TALL + nbb * SEQ:
                                                nhc * TALL + (nbb + 1) * SEQ]
                wt("scalar", "ve", arg)
                wt("scalar", "te", nstate["te_prev"])
                nstate["sqd"][s8] = ins("scalar", lambda e, sems, sr=srcn: e.activation(
                    sq_sb[:, :], sr, AF.Square), inc="se")
            else:
                tt = arg
                bank, bn2 = (p[4], "p4") if tt % 2 == 0 else (p[5], "p5")
                wt("tensor", "se", nstate["sqd"][s8])
                wt("tensor", "gpq", c_consts)
                use("tensor", bn2)
                tl = ins("tensor", lambda e, sems, t=tt, b=bank: e.matmul(
                    b[0:1, :], ones_col[:, :], sq_sb[:, t * 512:(t + 1) * 512],
                    start=True, stop=True), inc="te")
                if tt == 7:
                    nstate["te_prev"] = tl
                wt("vector", "te", tl)
                nstate["vl"] = ins("vector", lambda e, sems, s=s8, t=tt, b=bank: e.reduce_max(
                    nmax[0:1, s * 8 + t: s * 8 + t + 1], b[0:1, :], axis=AX.X), inc="ve")
                release(bn2, "ve")

    for r8 in range(NCORES):
        for c2 in range(2):
            lds = []
            for half in range(2):
                buf = hbuf % 3
                hbuf += 1
                if it == 0 and half == 0:
                    wt("sync", "cc0", 1)
                    wt("sync", "te", te_after_ffn)
                use("sync", f"xn_in{buf}")
                lds.append((buf, ins("sync", lambda e, sems, r=r8, c=c2, hh=half,
                                     b=buf: e.dma_start(
                    out=xn_in[b][:].rearrange("q (dc t) -> q dc t", dc=8),
                    in_=xn_all_v[r, :, hh * 8:(hh + 1) * 8, c * 512:(c + 1) * 512]),
                    inc="ldq", inc_by=16)))
            par = it % 2
            qb = ((p[0], "p0"), (p[1], "p1")) if par == 0 else ((p[4], "p4"), (p[5], "p5"))
            kb = ((p[2], "p2"), (p[3], "p3")) if par == 0 else ((p6f, "p6f"), (p7f, "p7f"))
            wt("tensor", "gpq", wk_ld)
            for _, n_ in qb + kb:
                use("tensor", n_)
            tl = None
            for half in range(2):
                buf, ld = lds[half]
                wt("tensor", "ldq", ld)
                for d8 in range(8):
                    dc = half * 8 + d8
                    for wi, (w_sb, bks) in enumerate(((wq_sb, qb), (wk_sb, kb))):
                        for j in range(2):
                            bk = bks[j][0]
                            tl = ins("tensor", lambda e, sems, d=dc, d8_=d8, jj=j, b=bk,
                                     bf=buf, w=w_sb: e.matmul(
                                b[:, :],
                                w[:, d * 256 + jj * 128:d * 256 + (jj + 1) * 128],
                                xn_in[bf][:, d8_ * 512:(d8_ + 1) * 512],
                                start=(d == 0), stop=(d == NDC - 1)),
                                inc="te" if (d8 == 7 and wi == 1 and j == 1) else None)
                release(f"xn_in{buf}", "te")
            wt("vector", "te", tl)
            for qk, (dst_sb, bks) in enumerate(((qt, qb), (kt_, kb))):
                for j in range(2):
                    bk, bn2 = bks[j]
                    dvv = dst_sb[:].rearrange("q (a c) -> q a c", c=128)
                    a0 = j * 64 + c2 * 32 + r8
                    dv = dvv[:, a0:a0 + 25:8, :]
                    vl = ins("vector", lambda e, sems, d=dv, b=bk: e.tensor_copy(
                        d, b[:].rearrange("q (b c) -> q b c", c=128)), inc="ve")
                    release(bn2, "ve")
            it += 1
    for s8 in range(8):
        nq.append(("sq", s8, C["ve"]))
        for tt in range(8):
            nq.append(("mm", s8, tt))

    cm7f.__exit__(None, None, None)
    cm6f.__exit__(None, None, None)
    cm6b = nc.psum_tensor("p6b", [128, 1024], BF16)
    cm7b = nc.psum_tensor("p7b", [128, 1024], BF16)
    p6b = cm6b.__enter__()
    p7b = cm7b.__enter__()
    bld.rel["p6b"] = bld.rel.get("p6f")
    bld.rel["p7b"] = bld.rel.get("p7f")

    vtr = 0
    for r8 in range(NCORES):
        for c2 in range(2):
            lds = []
            for half in range(2):
                buf = hbuf % 3
                hbuf += 1
                use("sync", f"xn_in{buf}")
                lds.append((buf, ins("sync", lambda e, sems, r=r8, c=c2, hh=half,
                                     b=buf: e.dma_start(
                    out=xn_in[b][:].rearrange("q (dc t) -> q dc t", dc=8),
                    in_=xn_all_v[r, :, hh * 8:(hh + 1) * 8, c * 512:(c + 1) * 512]),
                    inc="ldq", inc_by=16)))
            b0, b1 = (p[0], p[1]) if it % 2 == 0 else (p[2], p[3])
            n0, n1 = ("p0", "p1") if it % 2 == 0 else ("p2", "p3")
            wt("tensor", "gpq", wv_ld)
            use("tensor", n0)
            use("tensor", n1)
            tl = None
            for half in range(2):
                buf, ld = lds[half]
                wt("tensor", "ldq", ld)
                for d8 in range(8):
                    dc = half * 8 + d8
                    for j, bk in ((0, b0), (1, b1)):
                        tl = ins("tensor", lambda e, sems, d=dc, d8_=d8, jj=j, b=bk,
                                 bf=buf: e.matmul(
                            b[:, :],
                            wv_sb[:, d * 256 + jj * 128:d * 256 + (jj + 1) * 128],
                            xn_in[bf][:, d8_ * 512:(d8_ + 1) * 512],
                            start=(d == 0), stop=(d == NDC - 1)),
                            inc="te" if (d8 == 7 and j == 1) else None)
                release(f"xn_in{buf}", "te")
            wt("vector", "te", tl)
            for j, (bk, bn2) in enumerate(((b0, n0), (b1, n1))):
                use("vector", f"vtmp{j}")
                vl = ins("vector", lambda e, sems, b=bk, jj=j: e.tensor_copy(
                    vtmp[:, jj * 512:(jj + 1) * 512], b[:, :]), inc="ve")
                release(bn2, "ve")
                tb, tbn = (p6b, "p6b") if vtr % 2 == 0 else (p7b, "p7b")
                vtr += 1
                wt("tensor", "ve", vl)
                use("tensor", tbn)
                tl2 = None
                for k in range(4):
                    tl2 = ins("tensor", lambda e, sems, jj=j, kk=k, b=tb: e.transpose(
                        b[:, kk * 128:(kk + 1) * 128],
                        vtmp[:, jj * 512 + kk * 128: jj * 512 + (kk + 1) * 128],
                        ident[:, :]), inc="te" if k == 3 else None)
                release(f"vtmp{j}", "te")
                wt("vector", "te", tl2)
                dv = vtok[:].rearrange("q (tc c) -> q tc c", c=256)
                t0 = c2 * 32 + r8
                dsty = dv[:, t0:t0 + 25:8, j * 128:(j + 1) * 128]
                srcy = tb[:].rearrange("q (k c) -> q k c", k=8)
                ins("vector", lambda e, sems, d=dsty, s=srcy: e.tensor_copy(
                    d, s[:, 0:4, :]), inc="ve")
                release(tbn, "ve")
            it += 1
            drain_norm(5)
    drain_norm(len(nq))
    qkv_done_ve = C["ve"]
    cm7b.__exit__(None, None, None)
    cm6b.__exit__(None, None, None)
    p6f = stack.enter_context(nc.psum_tensor("p6f", [128, 512], F32))
    p7f = stack.enter_context(nc.psum_tensor("p7f", [128, 512], F32))
    bld.rel["p6f"] = bld.rel.get("p6b")
    bld.rel["p7f"] = bld.rel.get("p7b")

    wt("vector", "ve", nstate["vl"])
    for s8 in range(8):
        vl = ins("vector", lambda e, sems, s=s8: e.reduce_max(
            nmm[0:1, s:s + 1], nmax[0:1, s * 8:(s + 1) * 8], axis=AX.X),
            inc="ve" if s8 == 7 else None)
    wt("vector", "ve", vl)
    v4 = ins("vector", lambda e, sems: e.tensor_mul(
        c2s[0:1, :], nmm[0:1, 0:4], nmm[0:1, 4:8]), inc="ve")
    wt("scalar", "ve", v4)
    s2 = ins("scalar", lambda e, sems: e.activation(cs_[0:1, :], c2s[0:1, :], AF.Sqrt),
             inc="se")
    wt("scalar", "se", s2)
    s3 = ins("scalar", lambda e, sems: e.mul(negc[0:1, :], cs_[0:1, :], -SCALE * 1.02),
             inc="se")
    wt("tensor", "se", s3)
    use("tensor", "p4")
    tl = ins("tensor", lambda e, sems: e.matmul(
        p[4][:, 0:4], ones_row[:, :], negc[0:1, :], start=True, stop=True), inc="te")
    wt("scalar", "te", tl)
    s4 = ins("scalar", lambda e, sems: e.copy(bias_nc[:, :], p[4][:, 0:4]), inc="se")
    release("p4", "se")
    bias_ready = s4

    pt_rel = [None] * 7
    ostage_dma = {}
    ochunk_stq = {}
    first_att = True
    pending_fins = []
    fin_res_all = {}
    o_te_by_par = {0: 0, 1: 0}
    chunk_left = {c: 2 for c in range(8)}
    carry_st = {}
    o_items = []

    def o_slice(slot, dot):
        return o_stage[:, slot * 2048 + dot * 512: slot * 2048 + (dot + 1) * 512]

    def o_src(slot):
        return o_stage[:, slot * 2048:(slot + 1) * 2048]

    def make_opairs(bb, qc, qpar):
        cidx = bb * 4 + qc // 2
        rowc = (qc % 2) * 512
        cast_se = {}
        cast_ve = {}
        t_os = {}

        def em_mm(tnum):
            tsub, dot = tnum // 4, tnum % 4
            ob, on = (p[2], "p2") if tnum % 2 == 0 else (p[4], "p4")
            wt("tensor", "ve", fin_res_all[(qpar, 0)])
            wt("tensor", "ve", fin_res_all[(qpar, 1)])
            use("tensor", on)
            ins("tensor", lambda e, sems, b=ob, t=tsub, d=dot, qp=qpar: e.matmul(
                b[:, :], attnT[:, (qp * 2) * 512 + t * 128:(qp * 2) * 512 + (t + 1) * 128],
                woT_sb[:, d * 512:(d + 1) * 512],
                start=True, stop=False))
            t_o = ins("tensor", lambda e, sems, b=ob, t=tsub, d=dot, qp=qpar: e.matmul(
                b[:, :], attnT[:, (qp * 2 + 1) * 512 + t * 128:(qp * 2 + 1) * 512 + (t + 1) * 128],
                woT_sb[:, 2048 + d * 512: 2048 + (d + 1) * 512],
                start=False, stop=True), inc="te")
            o_te_by_par[qpar] = t_o
            t_os[tnum] = t_o

        def em_cast(tnum):
            tsub, dot = tnum // 4, tnum % 4
            ob, on = (p[2], "p2") if tnum % 2 == 0 else (p[4], "p4")
            slot = tsub % 2
            dsl = o_slice(slot, dot)
            if dot < 2:
                wt("scalar", "te", t_os[tnum])
                if ostage_dma.get(slot) is not None:
                    wt("scalar", "ostq", ostage_dma[slot])
                cast_se[tsub] = ins("scalar", lambda e, sems, d=dsl, b=ob: e.activation(
                    d, b[:, :], AF.Copy), inc="se")
                release(on, "se")
            else:
                wt("vector", "te", t_os[tnum])
                if ostage_dma.get(slot) is not None:
                    wt("vector", "ostq", ostage_dma[slot])
                cast_ve[tsub] = ins("vector", lambda e, sems, d=dsl, b=ob: e.tensor_copy(
                    d, b[:, :]), inc="ve")
                release(on, "ve")
            if dot == 3:
                wt("sync", "se", cast_se[tsub])
                wt("sync", "ve", cast_ve[tsub])
                dst = o_part[cidx][rowc + tsub * 128: rowc + (tsub + 1) * 128, :]
                sd = ins("sync", lambda e, sems, d=dst, s=o_src(slot): e.dma_start(
                    out=d, in_=s), inc="ostq", inc_by=16)
                ostage_dma[slot] = sd
                if tsub == 3:
                    chunk_left[cidx] -= 1
                    if chunk_left[cidx] == 0:
                        ochunk_stq[cidx] = C["ostq"]

        acts = []
        for k in range(16):
            acts.append(lambda n=k: em_mm(n))
            if k >= 1:
                acts.append(lambda n=k - 1: em_cast(n))
        acts.append(lambda: em_cast(15))
        return acts

    _order = [(bb, qc) for bb in range(B) for qc in (7, 6, 5, 4, 3, 2, 1, 0)]
    nxt_block = {a: b for a, b in zip(_order, _order[1:])}
    for bb in range(B):
        for qc in (7, 6, 5, 4, 3, 2, 1, 0):
            nkt = 4 * qc + 4
            qpar = qc % 2
            drain_left = [2 * (nkt - 2)]

            def drain_o(drain_left=drain_left):
                if drain_left[0] <= 0:
                    drain_left[0] -= 1
                    while o_items:
                        o_items.pop(0)()
                    return
                k = -(-len(o_items) // drain_left[0])
                drain_left[0] -= 1
                for _ in range(k):
                    if o_items:
                        o_items.pop(0)()

            for hc in range(HPC):
                bidx = bb * 2 + hc
                pvb, pvn = (p[5], "p5") if hc == 0 else (p6f, "p6f")
                dnb, dnn = (p[3], "p3") if hc == 0 else (p7f, "p7f")
                kslice = lambda k, h=hc, b=bb: kt_[:, h * TALL + b * SEQ + k * 128:
                                                   h * TALL + b * SEQ + (k + 1) * 128]
                qslice = qt[:, hc * TALL + bb * SEQ + qc * 512:
                            hc * TALL + bb * SEQ + (qc + 1) * 512]

                def emit_st(kti, hc=hc, qslice=qslice, kslice=kslice):
                    stb, stn = ((p[0], "p0"), (p[1], "p1"))[kti % 2]
                    use("tensor", stn)
                    return ins("tensor", lambda e, sems, b=stb, k=kslice(kti), q=qslice:
                               e.matmul(b[:, :], k, q, start=True, stop=True), inc="te")

                if first_att:
                    wt("tensor", "ve", qkv_done_ve)
                    wt("tensor", "se", bias_ready)
                    wt("tensor", "gpq", wo_ld)
                    first_att = False
                if hc == 0 and carry_st:
                    t_st = carry_st
                    carry_st = {}
                else:
                    t_st = {}
                    for kk in range(min(2, nkt)):
                        t_st[kk] = emit_st(kk)
                for kti in range(nkt):
                    stb, stn = ((p[0], "p0"), (p[1], "p1"))[kti % 2]
                    wt("scalar", "te", t_st[kti])
                    bld.we("scalar", pt_rel[kti % 7])
                    ex = ins("scalar", lambda e, sems, b=stb, sl=kti % 7, bi=bidx: e.activation(
                        pt[:, sl * 512:(sl + 1) * 512], b[:, :], AF.Exp,
                        bias=bias_nc[:, bi:bi + 1], scale=SCALE), inc="se")
                    release(stn, "se")
                    if kti >= 4 * qc:
                        a = kti - 4 * qc
                        wt("vector", "gpq", c_consts)
                        wt("vector", "se", ex)
                        mk = ins("vector", lambda e, sems, sl=kti % 7, aa=a: e.tensor_mul(
                            pt[:, sl * 512:(sl + 1) * 512],
                            pt[:, sl * 512:(sl + 1) * 512],
                            masks[:, aa * 512:(aa + 1) * 512]), inc="ve")
                        pt_ready = ("ve", mk)
                    else:
                        pt_ready = ("se", ex)
                    bld.we("tensor", pt_ready)
                    if kti == 0:
                        use("tensor", dnn)
                        use("tensor", pvn)
                    ins("tensor", lambda e, sems, sl=kti % 7, k0=(kti == 0), b=dnb,
                        kl=(kti == nkt - 1): e.matmul(
                        b[:, :], sel32[:, :], pt[:, sl * 512:(sl + 1) * 512],
                        start=k0, stop=kl))
                    vsl = vtok[:, (bb * 32 + kti) * 256 + hc * 128:
                               (bb * 32 + kti) * 256 + (hc + 1) * 128]
                    t_pv = ins("tensor", lambda e, sems, sl=kti % 7, v=vsl, k0=(kti == 0), b=pvb,
                               kl=(kti == nkt - 1): e.matmul(
                        b[:, :], v, pt[:, sl * 512:(sl + 1) * 512],
                        start=k0, stop=kl), inc="te")
                    pt_rel[kti % 7] = ("te", t_pv)
                    if kti + 2 < nkt:
                        t_st[kti + 2] = emit_st(kti + 2)
                    if kti == 0:
                        while pending_fins:
                            pending_fins.pop(0)()
                    elif kti >= 2:
                        drain_o()
                def make_fin(hc=hc, qp=qpar, pvb=pvb, pvn=pvn,
                             dnb=dnb, dnn=dnn, t_pv=t_pv):
                    def fin():
                        wt("scalar", "te", t_pv)
                        use("scalar", "recip")
                        s_ln = ins("scalar", lambda e, sems, b=dnb: e.activation(
                            recip[0:1, :], b[0:1, :], AF.Ln), inc="se")
                        release(dnn, "se")
                        wt("tensor", "se", s_ln)
                        use("tensor", dnn)
                        t_rb = ins("tensor", lambda e, sems: e.matmul(
                            dnb[:, :], ones_row[:, :], recip[0:1, :],
                            start=True, stop=True), inc="te")
                        release("recip", "te")
                        wt("scalar", "te", t_rb)
                        use("scalar", "rb_sb")
                        s_rb = ins("scalar", lambda e, sems: e.activation(
                            rb_sb[:, :], dnb[:, :], AF.Exp, scale=-1.0), inc="se")
                        release(dnn, "se")
                        wt("vector", "se", s_rb)
                        wt("vector", "te", t_pv)
                        wt("vector", "te", o_te_by_par[qp])
                        va = ins("vector", lambda e, sems: e.tensor_mul(
                            attnT[:, (qp * 2 + hc) * 512:(qp * 2 + hc + 1) * 512],
                            pvb[:, :], rb_sb[:, :]), inc="ve")
                        release(pvn, "ve")
                        release("rb_sb", "ve")
                        fin_res_all[(qp, hc)] = va
                    return fin
                pending_fins.append(make_fin())
            nxt = nxt_block.get((bb, qc))
            pre_st = {}
            if nxt is not None:
                nbb, nqc = nxt
                nk = lambda k, b=nbb: kt_[:, 0 * TALL + b * SEQ + k * 128:
                                          0 * TALL + b * SEQ + (k + 1) * 128]
                nq = qt[:, 0 * TALL + nbb * SEQ + nqc * 512:
                        0 * TALL + nbb * SEQ + (nqc + 1) * 512]
                for kk in range(min(2, 4 * nqc + 4)):
                    stb, stn = ((p[0], "p0"), (p[1], "p1"))[kk % 2]
                    use("tensor", stn)
                    pre_st[kk] = ins("tensor", lambda e, sems, b=stb, k=nk(kk), q=nq:
                                     e.matmul(b[:, :], k, q, start=True, stop=True),
                                     inc="te")
            carry_st = pre_st
            while o_items:
                o_items.pop(0)()
            o_items = make_opairs(bb, qc, qpar)
    while pending_fins:
        pending_fins.pop(0)()
    while o_items:
        o_items.pop(0)()

    chunk_order = [bb * 4 + k for bb in range(B) for k in (3, 2, 1, 0)]
    for n, cidx in enumerate(chunk_order):
        wt("gpsimd", "ostq", ochunk_stq[cidx])
        if n > 0:
            wt("gpsimd", f"cc{1 + chunk_order[n - 1]}", 1)
        ins("gpsimd", lambda e, sems, c=cidx: e.collective_compute(
            "ReduceScatter", ALU.add, [list(range(NCORES))],
            ins=[o_part[c]], outs=[rs_dst[c]]), inc=f"cc{1 + cidx}")
    for n, cidx in enumerate(chunk_order):
        wt("sync", f"cc{1 + cidx}", 1)
        use("sync", f"rs_in{n % 2}")
        ins("sync", lambda e, sems, c=cidx, jj=n % 2: e.dma_start(
            out=rs_in[:, jj * 2048:(jj + 1) * 2048], in_=rs_dst[c]),
            inc="ldq", inc_by=16)
        use("sync", f"x_ep{n % 2}")
        l2 = ins("sync", lambda e, sems, c=cidx, jj=n % 2: e.dma_start(
            out=x_ep[jj][:, :],
            in_=x_own[c * 128:(c + 1) * 128, :]),
            inc="ldq", inc_by=16)
        wt("gpsimd", f"cc{1 + cidx}", 1)
        wt("gpsimd", "ldq", l2)
        jj = n % 2
        use("gpsimd", f"oout{jj}")
        g1 = ins("gpsimd", lambda e, sems, c=cidx, j=jj: e.tensor_add(
            oout[j][:, :], rs_in[:, j * 2048:(j + 1) * 2048],
            ffn_tok[:, c * 2048:(c + 1) * 2048]), inc="gp")
        wt("gpsimd", "gp", g1)
        g2 = ins("gpsimd", lambda e, sems, j=jj: e.tensor_add(
            oout[j][:, :], oout[j][:, :], x_ep[j][:, :]), inc="gp")
        release(f"x_ep{jj}", "gp")
        release(f"rs_in{jj}", "gp")
        wt("scalar", "gp", g2)
        ins("scalar", lambda e, sems, c=cidx, j=jj: e.dma_start(
            out=out_parts[c, :, :], in_=oout[j][:, :]),
            inc="stq", inc_by=16)
        release(f"oout{jj}", "stq")
    if debug:
        for sname in ("te", "se", "ve", "gp"):
            wt("scalar", sname, C[sname])
        for i in range(9):
            wt("scalar", f"cc{i}", 1)
        wt("scalar", "stq", C["stq"])
        wt("scalar", "ldq", C["ldq"])
        taps = [
            ("dbg_xn_src", xn_src, [D, TOWN], BF16, "dram"),
            ("dbg_qt", qt, None, BF16, "sbuf"),
            ("dbg_kt", kt_, None, BF16, "sbuf"),
            ("dbg_vtok", vtok, None, BF16, "sbuf"),
            ("dbg_ffn", ffn_tok, None, BF16, "sbuf"),
            ("dbg_bias", bias_nc, None, F32, "sbuf"),
            ("dbg_nmm", nmm, None, F32, "sbuf"),
        ] + [(f"dbg_opart{c}", o_part[c], [1024, D], BF16, "dram") for c in range(8)] \
          + [(f"dbg_rs{c}", rs_dst[c], [128, D], BF16, "dram") for c in range(8)]
        for name, srcb, shp, dty, kind in taps:
            if kind == "dram":
                dst = nc.dram_tensor(name, shp, dty, kind="ExternalOutput").ap()
                ins("scalar", lambda e, sems, d=dst, s=srcb: e.dma_start(out=d, in_=s),
                    inc="stq", inc_by=16)
            else:
                shp2 = list(srcb.shape)
                dst = nc.dram_tensor(name, shp2, dty, kind="ExternalOutput").ap()
                ins("scalar", lambda e, sems, d=dst, s=srcb: e.dma_start(out=d, in_=s[:, :]),
                    inc="stq", inc_by=16)
    wt("scalar", "stq", C["stq"])
    wt("sync", "stq", C["stq"])
    wt("sync", "ostq", C["ostq"])

    with contextlib.ExitStack() as es:
        sems = {}
        for s in bld.semnames:
            sems[s] = es.enter_context(nc.semaphore(s))
        with nc.Block() as block:
            @block.sync
            def _(e):
                for fn in bld.streams["sync"]:
                    fn(e, sems)

            @block.scalar
            def _(e):
                for fn in bld.streams["scalar"]:
                    fn(e, sems)

            @block.vector
            def _(e):
                for fn in bld.streams["vector"]:
                    fn(e, sems)

            @block.tensor
            def _(e):
                for fn in bld.streams["tensor"]:
                    fn(e, sems)

            @block.gpsimd
            def _(e):
                for fn in bld.streams["gpsimd"]:
                    fn(e, sems)
    stack.close()
    return nc


def make_inputs(L0_attn_x, L0_attn_Wq, L0_attn_Wk, L0_attn_Wv, L0_attn_Wo,
                L0_ffn_gate_w, L0_ffn_up_w, L0_ffn_down_w,
                L0_ffn_gate_s, L0_ffn_up_s, L0_ffn_down_s):
    bf = ml_dtypes.bfloat16
    x = np.ascontiguousarray(np.asarray(L0_attn_x, dtype=np.float32).reshape(TALL, D))
    x4 = x.reshape(8, NCORES, 128, D)

    masks = np.zeros((128, 4 * 512), dtype=bf)
    for a in range(4):
        pp = np.arange(128)[:, None]
        jj = np.arange(512)[None, :]
        masks[:, a * 512:(a + 1) * 512] = (a * 128 + pp <= jj).astype(bf)
    ident = np.eye(128, dtype=bf)
    ones_col = np.ones((128, 1), dtype=bf)
    ones_row = np.ones((1, 128), dtype=np.float32)
    sel32 = np.zeros((128, 128), dtype=bf)
    sel32[:, 0] = 1.0
    ones_row_bf = ones_row.astype(bf)
    gs = np.ascontiguousarray(np.asarray(L0_ffn_gate_s, np.float32).reshape(NIC, 128).T)
    us = np.ascontiguousarray(np.asarray(L0_ffn_up_s, np.float32).reshape(NIC, 128).T)
    ds = np.ascontiguousarray(np.asarray(L0_ffn_down_s, np.float32).reshape(NDC, 128).T)
    gT = np.ascontiguousarray(np.asarray(L0_ffn_gate_w, np.float32)
                              .reshape(NIC, 128, NDC, 128).transpose(0, 3, 2, 1)
                              .reshape(NIC * 128, D))
    uT = np.ascontiguousarray(np.asarray(L0_ffn_up_w, np.float32)
                              .reshape(NIC, 128, NDC, 128).transpose(0, 3, 2, 1)
                              .reshape(NIC * 128, D))
    dT = np.ascontiguousarray(np.asarray(L0_ffn_down_w, np.float32)
                              .reshape(NDC, 128, NIC, 128).transpose(0, 3, 2, 1)
                              .reshape(NDC * 128, INTER))
    Wq = np.asarray(L0_attn_Wq, np.float32)
    Wk = np.asarray(L0_attn_Wk, np.float32)
    Wv = np.asarray(L0_attn_Wv, np.float32)
    WoT = np.ascontiguousarray(np.asarray(L0_attn_Wo, np.float32).T)

    def wtile(W, r):
        wo = W[2 * r * DH:(2 * r + 2) * DH, :]
        return np.ascontiguousarray(
            wo.reshape(HPC * DH, NDC, 128).transpose(2, 1, 0).reshape(128, NDC * 256))

    in_maps = []
    for r in range(NCORES):
        sl = slice(2 * r * DH, (2 * r + 2) * DH)
        in_maps.append({
            "x_own": np.ascontiguousarray(x4[:, r].reshape(TOWN, D)),
            "wqT": wtile(Wq, r),
            "wkT": wtile(Wk, r),
            "wvT": wtile(Wv, r),
            "woT": np.ascontiguousarray(WoT[sl, :]),
            "gateT": gT, "upT": uT, "downT": dT,
            "gs": gs, "us": us, "ds": ds,
            "masks": masks, "ident": ident,
            "ones_col": ones_col, "ones_row": ones_row, "sel32": sel32,
            "ones_row_bf": ones_row_bf,
        })
    return in_maps


def kernel(L0_attn_x, L0_attn_Wq, L0_attn_Wk, L0_attn_Wv, L0_attn_Wo,
           L0_ffn_gate_w, L0_ffn_up_w, L0_ffn_down_w,
           L0_ffn_gate_s, L0_ffn_up_s, L0_ffn_down_s):
    in_maps = make_inputs(L0_attn_x, L0_attn_Wq, L0_attn_Wk, L0_attn_Wv, L0_attn_Wo,
                          L0_ffn_gate_w, L0_ffn_up_w, L0_ffn_down_w,
                          L0_ffn_gate_s, L0_ffn_up_s, L0_ffn_down_s)
    nc = build_nc()
    res = run_bass_kernel_spmd(nc, in_maps, core_ids=list(range(NCORES)))
    out = np.empty((8, NCORES, 128, D), dtype=np.float32)
    for r in range(NCORES):
        out[:, r] = res.results[r]["out_parts"]
    return out.reshape(B, SEQ, D)
